# version 51
# baseline (speedup 1.0000x reference)
import os
os.environ.setdefault("TRN_ALLOW_ICI_ALL_TO_ALL", "1")
import contextlib
import numpy as np
import ml_dtypes
import concourse.bass as bass
import concourse.mybir as mybir
from concourse.bass_utils import run_bass_kernel_spmd

F32 = mybir.dt.float32
BF16 = mybir.dt.bfloat16
I32 = mybir.dt.int32
AF = mybir.ActivationFunctionType
ALU = mybir.AluOpType
AX = mybir.AxisListType

class _Step:
    __slots__ = ("name", "engine", "deps", "emit", "is_dma", "ndma",
                 "seq", "dma_hi", "marked", "idx")

    def __init__(self, name, engine, deps, emit, is_dma):
        self.name = name
        self.engine = engine
        self.deps = list(deps)
        self.emit = emit
        self.is_dma = is_dma
        self.ndma = 0
        self.seq = 0
        self.dma_hi = 0
        self.marked = False
        self.idx = 0


class Prog:
    ENGINES = ("sync", "gpsimd", "scalar", "vector", "tensor")

    def __init__(self, nc, ctx):
        self.nc = nc
        self.steps = []
        self.by_name = {}
        self.step_sem = {e: ctx.enter_context(nc.semaphore(f"ss_{e}"))
                         for e in self.ENGINES}
        self.dma_sem = {e: ctx.enter_context(nc.semaphore(f"sd_{e}"))
                        for e in ("sync", "gpsimd", "scalar", "vector")}

    def step(self, name, engine, deps=()):
        def reg(fn):
            s = _Step(name, engine, deps, fn, False)
            assert name not in self.by_name, name
            self.by_name[name] = s
            self.steps.append(s)
            return name
        return reg

    def dma(self, name, engine, deps=()):
        def reg(fn):
            s = _Step(name, engine, deps, fn, True)
            assert name not in self.by_name, name
            self.by_name[name] = s
            self.steps.append(s)
            return name
        return reg

    def build(self, block, dma_counts):
        for s in self.steps:
            for d in s.deps:
                ds = self.by_name[d]
                if ds.engine != s.engine and not ds.is_dma:
                    ds.marked = True
        cnt = {e: 0 for e in self.ENGINES}
        dcnt = {e: 0 for e in self.ENGINES}
        for i, s in enumerate(self.steps):
            s.idx = i
            if s.is_dma:
                s.ndma = dma_counts[s.name]
                dcnt[s.engine] += s.ndma
                s.dma_hi = dcnt[s.engine]
            else:
                if s.marked:
                    cnt[s.engine] += 1
                    s.seq = cnt[s.engine]
        prog = self

        def engine_fn(engine):
            def run(eng):
                waited = {}

                def need(sem, val):
                    key = id(sem)
                    if waited.get(key, -1) < val:
                        eng.wait_ge(sem, val)
                        waited[key] = val

                ndma_issued = [0]

                def dma(out, in_, **kw):
                    ndma_issued[0] += 1
                    return eng.dma_start(out=out, in_=in_, **kw).then_inc(
                        prog.dma_sem[engine], 16)

                for s in prog.steps:
                    if s.engine != engine:
                        continue
                    for d in s.deps:
                        ds = prog.by_name[d]
                        if ds.is_dma:
                            need(prog.dma_sem[ds.engine], 16 * ds.dma_hi)
                        elif ds.engine != engine:
                            need(prog.step_sem[ds.engine], ds.seq)
                    before = ndma_issued[0]
                    if s.is_dma:
                        s.emit(eng, dma)
                        assert ndma_issued[0] - before == s.ndma, (
                            f"{s.name}: declared {s.ndma}, issued "
                            f"{ndma_issued[0] - before}")
                    else:
                        last = s.emit(eng)
                        if s.marked:
                            if last is None:
                                eng.engine_nop().then_inc(prog.step_sem[engine], 1)
                            else:
                                last.then_inc(prog.step_sem[engine], 1)
            return run

        for e in self.ENGINES:
            getattr(block, e)(engine_fn(e))


H, N, Q, J, L = 128, 64, 128, 128, 8192
NTOK = 2 * L
INV2PI = float(1.0 / (2 * np.pi))
TWOPI = float(2 * np.pi)
NQT = 4
QW = H * N // NQT
NG = 8
T = 2048
TH = T + 1
CHUNKS = [("h", 0, 1), ("0", 1, 512), ("1", 513, 512), ("2", 1025, 512),
          ("3", 1537, 512)]
FCHUNKS = [("0", 0, 512), ("1", 512, 512), ("2", 1024, 512), ("3", 1536, 512)]

CCOL_NP = np.zeros((128, 8), dtype=np.float32)
CCOL_NP[:, 0] = np.arange(128) + 1
CCOL_NP[:, 1] = np.pi / 2
CCOL_NP[:, 3] = 1.0
ANTI_NP = np.eye(128, dtype=ml_dtypes.bfloat16)[::-1].copy()
IDENT_NP = np.eye(128, dtype=np.float32)
IOTA_NP = np.broadcast_to((np.arange(128, dtype=np.float32) + 1)[:, None],
                          (128, 2048)).copy()

AR_W = 47120
MAIN0 = 20480


def build_conv(nc, ctx, p, io, dma_counts, arena, ps):
    sb = lambda name, shape, dt_: ctx.enter_context(nc.sbuf_tensor(name, shape, dt_))

    def f32v(off, w):
        return arena[:, off:off + w]

    def bf16v(off, w_f32):
        return arena[:, off:off + w_f32].bitcast(BF16)

    ere = bf16v(0, 4096)
    eim = bf16v(4096, 4096)
    p2re = bf16v(8192, 4096)
    p2im = bf16v(12288, 4096)
    wbr = p2re
    wbi = p2im
    GEN0 = 16384
    g_rep = f32v(GEN0 + 0 * QW, QW)
    g_arg = f32v(GEN0 + 1 * QW, QW)
    g_mag = f32v(GEN0 + 2 * QW, QW)
    g_t1 = f32v(GEN0 + 3 * QW, QW)
    g_t2 = f32v(GEN0 + 4 * QW, QW)
    g_t3 = f32v(GEN0 + 5 * QW, QW)
    g_i = f32v(GEN0 + 6 * QW, QW).bitcast(I32)
    g_tre = f32v(GEN0 + 7 * QW, QW)
    g_tim = f32v(GEN0 + 8 * QW, QW)
    xf = bf16v(GEN0, 8192)
    toep = bf16v(GEN0 + 8192, 8192)
    XT0 = GEN0 + 18432
    xt = bf16v(XT0, 8192)
    ure = bf16v(XT0, 4096)
    uim = bf16v(XT0 + 4096, 4096)
    t_iota = f32v(43008, 2048)
    u_stage = [f32v(45056, 1024), f32v(46080, 1024)]
    ybuf = [bf16v(38912, 512), bf16v(39424, 512)]

    t_ccol = sb("t_ccol", [128, 8], F32)
    t_are = sb("t_are", [H, N], F32)
    t_aim = sb("t_aim", [H, N], F32)
    t_ldt = sb("t_ldt", [H, 1], F32)
    t_cre = sb("t_cre", [H, N], F32)
    t_cim = sb("t_cim", [H, N], F32)
    t_anti = sb("t_anti", [128, 128], BF16)
    t_id = sb("t_id", [128, 128], F32)
    t_wq = sb("t_wq", [128, 12 * 64], F32)
    t_w = [t_wq[:, i * 64:(i + 1) * 64] for i in range(12)]
    t_wi = sb("t_wi", [H, N], I32)
    t_wi2 = sb("t_wi2", [H, N], I32)
    t_dmy = sb("t_dmy", [H, N], F32)
    t_dt = sb("t_dt", [H, 1], F32)
    t_k0 = sb("t_k0", [H, 1], F32)
    krow = sb("krow", [128, 256], F32)
    ksub = sb("ksub", [128, 128], F32)
    lqr_t = sb("lqr_t", [128, 128], F32)
    lqi_t = sb("lqi_t", [128, 128], F32)
    lqr = lqr_t[:, 0:64]
    lqi = lqi_t[:, 0:64]
    s_q = f32v(43008, 512)
    s_ = [s_q[:, i * 128:(i + 1) * 128] for i in range(4)]
    s3 = [t.rearrange("p (a b) -> p a b", a=64) for t in s_]
    wst_q = f32v(43520, 768)
    wst = [wst_q[:, i * 128:(i + 1) * 128] for i in range(6)]
    t_lam2 = sb("t_lam2", [128, 128], F32)
    t_lam3 = sb("t_lam3", [128, 128], F32)
    t_lqs = sb("t_lqs", [128, 128], F32)
    t_lqs2 = sb("t_lqs2", [128, 128], F32)
    zcol = sb("zcol", [128, 2], BF16)

    ps_a, ps_b = ps[0], ps[1]
    ps_u = ps[2:6]
    ps_y = ps[6:8]

    nc.const_aps.register(F32, 0.0, t_ccol[:, 2:3])
    nc.const_aps.register(F32, 1.0, t_ccol[:, 3:4])

    io_iota = t_iota
    ccol = io["ccol"]; are = io["are"]; aim = io["aim"]; ldt = io["ldt"]
    cre = io["cre"]; cim = io["cim"]; anti = io["anti"]; ident = io["ident"]
    xc = io["xc"]; dpar = io["dpar"]; dkpad = io["dkpad"]; dyb = io["dyb"]
    dlq = io["dlq"]

    ure_r = ure.rearrange("p (a b) -> p a b", a=64)
    uim_r = uim.rearrange("p (a b) -> p a b", a=64)
    wbr_r = wbr.rearrange("p (a b) -> p a b", a=64)
    wbi_r = wbi.rearrange("p (a b) -> p a b", a=64)
    lqr_b = lqr.unsqueeze(2).broadcast_to([128, 64, 2])
    lqi_b = lqi.unsqueeze(2).broadcast_to([128, 64, 2])
    wst3 = [t.rearrange("p (a b) -> p a b", a=64) for t in wst]
    wst3_r = [wst3[0], wst3[2], wst3[4]]
    wst3_i = [wst3[1], wst3[3], wst3[5]]

    @p.dma("ld_par", "sync")
    def _(eng, dma):
        for t_, d_ in [(t_ccol, ccol), (t_are, are), (t_aim, aim), (t_ldt, ldt),
                       (t_cre, cre), (t_cim, cim), (t_id, ident), (t_anti, anti)]:
            dma(t_[:], d_.ap())
        dma(t_iota, io["iota_t"].ap())
    dma_counts["ld_par"] = 9

    @p.dma("ld_x", "gpsimd", deps=[])
    def _(eng, dma):
        for jr in range(8):
            src = bass.AP(xc, (jr * 16) * 128 * 128,
                          [[128, 128], [128 * 128, 16], [1, 128]])
            dst = xt[:, jr * 16 * 128:(jr + 1) * 16 * 128]
            dma(dst.rearrange("p (a b) -> p a b", a=16), src)
    dma_counts["ld_x"] = 8

    @p.step("sc_dt", "scalar", deps=["ld_par"])
    def _(eng):
        return eng.activation(t_dt[:], t_ldt[:], AF.Exp)

    def round_reduce_pair(v, src, s1, o1, s2, o2, tfa, tia, tfb, tib):
        TS = v.tensor_scalar
        TS(tfa[:], src[:], INV2PI, s1, ALU.mult, ALU.add)
        TS(tfb[:], src[:], INV2PI, s2, ALU.mult, ALU.add)
        v.tensor_copy(tia[:], tfa[:])
        v.tensor_copy(tib[:], tfb[:])
        v.tensor_copy(tfa[:], tia[:])
        v.tensor_copy(tfb[:], tib[:])
        v.scalar_tensor_tensor(o1[:], tfa[:], -TWOPI, src[:], ALU.mult, ALU.add)
        return v.scalar_tensor_tensor(o2[:], tfb[:], -TWOPI, src[:], ALU.mult, ALU.add)

    @p.step("prep1", "vector", deps=["ld_par", "sc_dt"])
    def _(v):
        TT, TS = v.tensor_tensor, v.tensor_scalar
        v.tensor_copy(t_dmy[:], t_dt[:].broadcast_to([H, N]))
        TS(t_w[0][:], t_are[:], t_dt[:], None, ALU.mult)
        TT(t_w[0][:], t_are[:], t_dmy[:], ALU.mult)
        TT(t_w[1][:], t_aim[:], t_dmy[:], ALU.mult)
        v.memset(zcol[:], 0.0)
        return round_reduce_pair(v, t_w[1], 0.0, t_w[6], 0.25, t_w[7],
                                 t_w[10], t_wi, t_w[11], t_wi2)

    @p.step("sc_trig1", "scalar", deps=["prep1"])
    def _(eng):
        eng.activation(t_w[4][:], t_w[0][:], AF.Exp)
        eng.activation(t_w[5][:], t_w[6][:], AF.Sin)
        return eng.activation(t_w[6][:], t_w[7][:], AF.Sin, bias=t_ccol[:, 1:2])

    @p.step("prep2", "vector", deps=["sc_trig1"])
    def _(v):
        TT, TS = v.tensor_tensor, v.tensor_scalar

        def SP():
            v.tensor_copy(t_dmy[:], t_w[0][:])
        TT(t_w[8][:], t_w[4][:], t_w[6][:], ALU.mult)
        TT(t_w[9][:], t_w[4][:], t_w[5][:], ALU.mult)
        TT(t_w[4][:], t_are[:], t_are[:], ALU.mult)
        TT(t_w[11][:], t_aim[:], t_aim[:], ALU.mult)
        TS(t_w[10][:], t_w[8][:], 1.0, None, ALU.subtract)
        TT(t_w[4][:], t_w[4][:], t_w[11][:], ALU.add)
        TT(t_w[6][:], t_w[10][:], t_are[:], ALU.mult)
        TT(t_w[7][:], t_w[9][:], t_aim[:], ALU.mult)
        v.reciprocal(t_w[5][:], t_w[4][:])
        TT(t_w[6][:], t_w[6][:], t_w[7][:], ALU.add)
        TT(t_w[8][:], t_w[9][:], t_are[:], ALU.mult)
        TT(t_w[7][:], t_w[10][:], t_aim[:], ALU.mult)
        TT(t_w[6][:], t_w[6][:], t_w[5][:], ALU.mult)
        TT(t_w[8][:], t_w[8][:], t_w[7][:], ALU.subtract)
        TT(t_w[10][:], t_cre[:], t_w[6][:], ALU.mult)
        TT(t_w[8][:], t_w[8][:], t_w[5][:], ALU.mult)
        TT(t_w[7][:], t_cre[:], t_w[8][:], ALU.mult)
        SP()
        TT(t_w[11][:], t_cim[:], t_w[8][:], ALU.mult)
        TT(t_w[9][:], t_cim[:], t_w[6][:], ALU.mult)
        TT(t_w[2][:], t_w[10][:], t_w[11][:], ALU.subtract)
        TT(t_w[3][:], t_w[7][:], t_w[9][:], ALU.add)
        TS(t_w[2][:], t_w[2][:], 2.0, None, ALU.mult)
        TS(t_w[3][:], t_w[3][:], 2.0, None, ALU.mult)
        SP()
        v.reduce_sum(t_k0[:], t_w[2][:], AX.X)
        TS(t_w[7][:], t_w[0][:], 128.0, None, ALU.mult)
        TS(t_w[4][:], t_w[1][:], 128.0, None, ALU.mult)
        SP()
        return round_reduce_pair(v, t_w[4], 0.0, t_w[8], 0.25, t_w[9],
                                 t_w[10], t_wi, t_w[11], t_wi2)

    @p.step("sc_trig2", "scalar", deps=["prep2"])
    def _(eng):
        eng.activation(t_w[7][:], t_w[7][:], AF.Exp)
        eng.activation(t_w[8][:], t_w[8][:], AF.Sin)
        return eng.activation(t_w[9][:], t_w[9][:], AF.Sin, bias=t_ccol[:, 1:2])

    @p.step("c_ones_i", "vector", deps=["ld_par"])
    def _(v):
        return v.memset(krow[0:1, 0:128], 1.0)

    @p.step("prep3", "vector", deps=["sc_trig2"])
    def _(v):
        v.tensor_tensor(t_w[4][:], t_w[7][:], t_w[9][:], ALU.mult)
        v.tensor_tensor(t_w[5][:], t_w[7][:], t_w[8][:], ALU.mult)
        v.memset(t_lam2[0:64, 64:128], 0.0)
        v.memset(t_lam2[64:128, 0:64], 0.0)
        v.memset(t_lam3[0:64, 64:128], 0.0)
        v.memset(t_lam3[64:128, 0:64], 0.0)
        v.tensor_copy(t_lam2[64:128, 64:128], t_w[4][64:128, :])
        v.tensor_copy(t_lam3[64:128, 64:128], t_w[5][64:128, :])
        v.tensor_copy(t_lam3[0:64, 0:64], t_w[5][0:64, :])
        return v.tensor_copy(t_lam2[0:64, 0:64], t_w[4][0:64, :])

    @p.dma("bounce", "sync", deps=["prep3"])
    def _(eng, dma):
        for i, t_ in enumerate([t_w[0], t_w[1], t_w[2], t_w[3]]):
            dma(bass.AP(dpar, i * 8192, [[64, 128], [1, 64]]), t_[:])
    dma_counts["bounce"] = 4

    @p.step("t_lamq", "tensor", deps=["prep3", "ld_par"])
    def _(te):
        te.transpose(ps_a[:, 0:128], t_lam2[:], t_id[:])
        return te.transpose(ps_b[:, 0:128], t_lam3[:], t_id[:])

    @p.step("v_lamq0", "vector", deps=["t_lamq"])
    def _(v):
        v.tensor_copy(t_lqs[:], ps_a[:, 0:128])
        v.tensor_copy(t_lqs2[:], ps_b[:, 0:128])
        v.tensor_copy(lqr_t[0:64, 0:64], t_lqs[0:64, 0:64])
        v.tensor_copy(lqi_t[0:64, 0:64], t_lqs2[0:64, 0:64])
        v.tensor_copy(lqr_t[64:128, 0:64], t_lqs[64:128, 64:128])
        return v.tensor_copy(lqi_t[64:128, 0:64], t_lqs2[64:128, 64:128])

    @p.dma("v_lamq", "sync", deps=["v_lamq0"])
    def _(eng, dma):
        dma(bass.AP(dlq, 0, [[64, 128], [1, 64]]), lqr)
    dma_counts["v_lamq"] = 1

    row_sl = [arena[0:1, 43008 + 512 * j:43008 + 512 * (j + 1)] for j in range(4)]
    c_ones = krow[0:1, 0:128]
    gv4 = lambda t: t.rearrange("p (i b n) -> p i b n", i=16, b=2)
    def pview(t, j):
        b, half = j // 2, j % 2
        return gv4(t)[:, 8 * half:8 * half + 8, b]

    def prow(T, qt, j):
        b, half = j // 2, j % 2
        return bass.AP(dpar, T * 8192 + (b * 64 + qt * 16 + 8 * half) * 64,
                       [[1, 512]])

    psrc3 = lambda pc: ps[2 + pc][:, 0:512].rearrange("p (a n) -> p a n", a=8)

    for qt in range(NQT):
        qx = str(qt)
        rbase = qt * 16 * 64

        @p.dma(f"rs_s{qx}", "sync",
               deps=["bounce"] + ([f"bcci{qt - 1}"] if qt else []))
        def _(eng, dma, qt=qt):
            for pc in range(4):
                dma(row_sl[pc], prow(0, qt, pc))
        dma_counts[f"rs_s{qx}"] = 4

        @p.step(f"bcs{qx}", "tensor",
                deps=[f"rs_s{qx}", "ld_par", "c_ones_i"] +
                ([f"genQ2{qt - 1}"] if qt else []))
        def _(te):
            for pc in range(4):
                last = te.matmul(ps[2 + pc][:, 0:512], c_ones,
                                 row_sl[pc][:, 0:512], start=True, stop=True)
            return last

        @p.step(f"genA{qx}", "scalar", deps=[f"bcs{qx}"] +
                ([f"genQ2{qt - 1}"] if qt else []))
        def _(eng):
            for pc in range(4):
                last = eng.activation(pview(g_arg, pc), psrc3(pc),
                                      AF.Copy, scale=t_ccol[:, 0:1])
            return last

        @p.dma(f"rs_t{qx}", "sync", deps=[f"bcs{qx}"])
        def _(eng, dma, qt=qt):
            for pc in range(4):
                dma(row_sl[pc], prow(1, qt, pc))
        dma_counts[f"rs_t{qx}"] = 4

        @p.step(f"bct{qx}", "tensor", deps=[f"rs_t{qx}", f"genA{qx}"])
        def _(te):
            for pc in range(4):
                last = te.matmul(ps[2 + pc][:, 0:512], c_ones,
                                 row_sl[pc][:, 0:512], start=True, stop=True)
            return last

        @p.step(f"genBt{qx}", "scalar", deps=[f"bct{qx}"] +
                ([f"genT{qt - 1}"] if qt else []))
        def _(eng):
            for pc in range(4):
                last = eng.activation(pview(g_t1, pc), psrc3(pc),
                                      AF.Copy, scale=t_ccol[:, 0:1])
            return last

        @p.step(f"genB{qx}", "vector", deps=[f"genBt{qx}"])
        def _(v):
            TS, STT = v.tensor_scalar, v.scalar_tensor_tensor
            g_i2 = g_mag.bitcast(I32)
            TS(g_t2, g_t1, INV2PI, 0.0, ALU.mult, ALU.add)
            TS(g_t3, g_t1, INV2PI, 0.25, ALU.mult, ALU.add)
            v.tensor_copy(g_i, g_t2)
            v.tensor_copy(g_i2, g_t3)
            v.tensor_copy(g_t2, g_i)
            v.tensor_copy(g_t3, g_i2)
            STT(g_t2, g_t2, -TWOPI, g_t1, ALU.mult, ALU.add)
            return STT(g_t3, g_t3, -TWOPI, g_t1, ALU.mult, ALU.add)

        @p.step(f"genT{qx}", "scalar", deps=[f"genB{qx}", f"genA{qx}"])
        def _(eng):
            eng.activation(g_mag, g_arg, AF.Exp)
            eng.activation(g_t3, g_t3, AF.Sin, bias=t_ccol[:, 1:2])
            return eng.activation(g_t2, g_t2, AF.Sin)

        @p.step(f"genP{qx}", "vector", deps=[f"genT{qx}"] +
                ([] if qt == 0 else [f"tE{qt - 1}_7"]))
        def _(v):
            v.tensor_tensor(g_tre, g_mag, g_t3, ALU.mult)
            return v.tensor_tensor(g_tim, g_mag, g_t2, ALU.mult)

        @p.dma(f"rs_cr{qx}", "sync", deps=[f"bct{qx}"])
        def _(eng, dma, qt=qt):
            for pc in range(4):
                dma(row_sl[pc], prow(2, qt, pc))
        dma_counts[f"rs_cr{qx}"] = 4

        @p.step(f"bccr{qx}", "tensor", deps=[f"rs_cr{qx}", f"genBt{qx}"])
        def _(te):
            for pc in range(4):
                last = te.matmul(ps[2 + pc][:, 0:512], c_ones,
                                 row_sl[pc][:, 0:512], start=True, stop=True)
            return last

        @p.step(f"genQ1{qx}", "vector", deps=[f"bccr{qx}", f"genP{qx}"])
        def _(v):
            for pc in range(4):
                v.tensor_tensor(pview(g_arg, pc), psrc3(pc), pview(g_tre, pc),
                                ALU.mult)
            for pc in range(4):
                last = v.tensor_tensor(pview(g_rep, pc), psrc3(pc),
                                       pview(g_tim, pc), ALU.mult)
            return last

        @p.dma(f"rs_ci{qx}", "sync", deps=[f"bccr{qx}"])
        def _(eng, dma, qt=qt):
            for pc in range(4):
                dma(row_sl[pc], prow(3, qt, pc))
        dma_counts[f"rs_ci{qx}"] = 4

        @p.step(f"bcci{qx}", "tensor", deps=[f"rs_ci{qx}", f"genQ1{qx}"])
        def _(te):
            for pc in range(4):
                last = te.matmul(ps[2 + pc][:, 0:512], c_ones,
                                 row_sl[pc][:, 0:512], start=True, stop=True)
            return last

        @p.step(f"genQ2{qx}", "vector", deps=[f"bcci{qx}"])
        def _(v, qt=qt):
            TT = v.tensor_tensor
            for pc in range(4):
                TT(pview(g_mag, pc), psrc3(pc), pview(g_tim, pc), ALU.mult)
            for pc in range(4):
                TT(pview(g_t3, pc), psrc3(pc), pview(g_tre, pc), ALU.mult)
            sl = lambda t, b: t.rearrange("p (i b n) -> p b i n", i=16, b=2)[:, b]
            for b in range(2):
                co = b * 4096 + qt * 1024
                TT(p2re[:, co:co + 1024].rearrange("p (i n) -> p i n", i=16),
                   sl(g_arg, b), sl(g_mag, b), ALU.subtract)
                last = TT(p2im[:, co:co + 1024].rearrange("p (i n) -> p i n", i=16),
                          sl(g_rep, b), sl(g_t3, b), ALU.add)
            return last

        for gidx in range(8):
            war = []
            if not (qt == 0 and gidx < 2):
                war = ([f"eE{qt}_{gidx - 2}"] if gidx >= 2 else [f"eE{qt - 1}_{gidx + 6}"])
            psx = [ps_a, ps_b][gidx % 2]

            @p.step(f"tE{qx}_{gidx}", "tensor",
                    deps=[f"genP{qx}", "v_lamq"] + war)
            def _(te, gidx=gidx, psx=psx):
                for k in range(2):
                    i = gidx * 2 + k
                    te.transpose(psx[:, k * 256:k * 256 + 128],
                                 g_tre[:, i * 128:(i + 1) * 128], t_id[:])
                    te.transpose(psx[:, k * 256 + 128:k * 256 + 256],
                                 g_tim[:, i * 128:(i + 1) * 128], t_id[:])
                return te.matmul(ps_y[0][:, 0:128], t_anti[:], xt[:, 0:128],
                                 start=True, stop=True)

            @p.step(f"eE{qx}_{gidx}", "scalar", deps=[f"tE{qx}_{gidx}"])
            def _(eng, qt=qt, gidx=gidx, psx=psx):
                for k in range(2):
                    i = gidx * 2 + k
                    col = (qt * 16 + i) * 128
                    eng.activation(ere[:, col + 1:col + 128],
                                   psx[:, k * 256:k * 256 + 127], AF.Copy)
                    last = eng.activation(eim[:, col + 1:col + 128],
                                          psx[:, k * 256 + 128:k * 256 + 255],
                                          AF.Copy, scale=-1.0)
                return last

    @p.step("einit", "vector", deps=["eE3_7"])
    def _(v):
        ere_r = ere.rearrange("p (a b) -> p a b", a=64)
        eim_r = eim.rearrange("p (a b) -> p a b", a=64)
        v.memset(ere_r[:, :, 0:1], 1.0)
        return v.memset(eim_r[:, :, 0:1], 0.0)

    @p.step("genK", "vector", deps=["genQ23", "prep2", "bcci3"])
    def _(v):
        v.reduce_sum(ksub[:], p2re.rearrange("p (h n) -> p h n", h=H), AX.X)
        v.memset(krow[:, 0:127], 0.0)
        return v.tensor_copy(krow[:, 127:128], t_k0[:])

    @p.step("t_ksub", "tensor", deps=["genK", "eE3_7"])
    def _(te):
        te.transpose(ps_a[:, 0:128], ksub[:], t_id[:])
        return te.matmul(ps_a[:, 256:384], t_anti[:], xt[:, 0:128],
                         start=True, stop=True)

    @p.step("e_ksub", "scalar", deps=["t_ksub"])
    def _(eng):
        return eng.activation(krow[:, 128:256], ps_a[:, 0:128], AF.Copy)

    @p.dma("w_kpad", "sync", deps=["e_ksub", "genK"])
    def _(eng, dma):
        dma(dkpad.ap(), krow[:])
    dma_counts["w_kpad"] = 1

    @p.dma("toepd", "gpsimd", deps=["w_kpad", "tE3_7", "genQ23"])
    def _(eng, dma):
        for h in range(H):
            src = bass.AP(dkpad, h * 256, [[1, 128], [1, 128]])
            dma(toep[:, h * 128:(h + 1) * 128], src)
    dma_counts["toepd"] = 128

    for i in range(32):
        war = ["genQ23", "e_ksub"] if i < 2 else [f"exf{i - 2}"]
        psx = [ps_a, ps_b][i % 2]

        @p.step(f"txf{i}", "tensor", deps=["ld_x", "ld_par"] + war)
        def _(te, i=i, psx=psx):
            return te.matmul(psx[:], t_anti[:], xt[:, i * 512:(i + 1) * 512],
                             start=True, stop=True)

        @p.step(f"exf{i}", "scalar", deps=[f"txf{i}"])
        def _(eng, i=i, psx=psx):
            return eng.activation(xf[:, i * 512:(i + 1) * 512], psx[:], AF.Copy)

    for hq in range(32):
        h0 = hq * 4
        grp = 0 if h0 < 64 else 1
        war = [] if hq < 2 else [f"ue{hq - 2}"]
        if hq >= 18 and grp == 1:
            war += [f"um{hq - 2}"]
        pu_re = ps_u[(hq % 2) * 2]
        pu_im = ps_u[(hq % 2) * 2 + 1]

        @p.step(f"u{hq}", "tensor", deps=["exf31"] + war)
        def _(te, h0=h0, pu_re=pu_re, pu_im=pu_im):
            for k in range(4):
                h = h0 + k
                p2r_s = p2re[:, h * 64:(h + 1) * 64]
                p2i_s = p2im[:, h * 64:(h + 1) * 64]
                xf_s = xf[:, h::128]
                te.matmul(pu_re[0:64, k * 128:(k + 1) * 128],
                          p2r_s, xf_s, start=True, stop=True)
                last = te.matmul(pu_im[0:64, k * 128:(k + 1) * 128],
                                 p2i_s, xf_s, start=True, stop=True)
            return last

        if grp == 0:
            @p.step(f"ue{hq}", "vector", deps=[f"u{hq}"])
            def _(v, h0=h0, pu_re=pu_re, pu_im=pu_im):
                slot = (h0 % 64) * 128
                v.tensor_copy(ure[0:64, slot:slot + 512], pu_re[0:64, :])
                return v.tensor_copy(uim[0:64, slot:slot + 512], pu_im[0:64, :])
        else:
            par = hq % 2

            @p.step(f"ue{hq}", "vector", deps=[f"u{hq}"])
            def _(v, pu_re=pu_re, pu_im=pu_im, par=par):
                v.tensor_copy(u_stage[0][0:64, par * 512:(par + 1) * 512],
                              pu_re[0:64, :])
                return v.tensor_copy(u_stage[1][0:64, par * 512:(par + 1) * 512],
                                     pu_im[0:64, :])

            @p.dma(f"um{hq}", "gpsimd", deps=[f"ue{hq}"])
            def _(eng, dma, h0=h0, par=par):
                slot = (h0 % 64) * 128
                dma(ure[64:128, slot:slot + 512],
                    u_stage[0][0:64, par * 512:(par + 1) * 512])
                dma(uim[64:128, slot:slot + 512],
                    u_stage[1][0:64, par * 512:(par + 1) * 512])
            dma_counts[f"um{hq}"] = 2

    @p.step("scan_in", "vector", deps=["ue31", "u31", "um31", "v_lamq"])
    def _(v):
        v.memset(wbr_r[:, :, 0::64], 0.0)
        v.memset(wbi_r[:, :, 0::64], 0.0)
        v.memset(wst[0][:], 0.0)
        return v.memset(wst[1][:], 0.0)

    for jb in range(1, 64):
        g = jb - 1
        gdeps = ["scan_in"] if g < 3 else [f"scpy{g - 3}"]

        @p.step(f"scan{g}", "vector", deps=gdeps)
        def _(v, jb=jb, g=g):
            TT = v.tensor_tensor
            if g == 0:
                wp_r3, wp_i3 = wst3[0], wst3[1]
            else:
                ps_ = (g - 1) % 3
                wp_r3, wp_i3 = wst3_r[ps_], wst3_i[ps_]
            ns_ = g % 3
            wn_r3, wn_i3 = wst3_r[ns_], wst3_i[ns_]
            up_r = ure_r[:, :, jb - 1::64]
            up_i = uim_r[:, :, jb - 1::64]
            TT(s3[0], wp_r3, lqr_b, ALU.mult)
            TT(s3[1], wp_i3, lqi_b, ALU.mult)
            TT(s3[2], wp_r3, lqi_b, ALU.mult)
            TT(s3[3], wp_i3, lqr_b, ALU.mult)
            TT(s3[0], s3[0], s3[1], ALU.subtract)
            TT(s3[2], s3[2], s3[3], ALU.add)
            TT(wn_r3, s3[0], up_r, ALU.add)
            return TT(wn_i3, s3[2], up_i, ALU.add)

        @p.step(f"scpy{g}", "scalar", deps=[f"scan{g}"])
        def _(eng, jb=jb, g=g):
            ns_ = g % 3
            eng.activation(wbr_r[:, :, jb::64], wst3_r[ns_], AF.Copy)
            return eng.activation(wbi_r[:, :, jb::64], wst3_i[ns_], AF.Copy)

    @p.step("scan", "vector", deps=["scpy62"])
    def _(v):
        return v.tensor_copy(t_dmy[:], t_dt[:].broadcast_to([H, N]))

    for hq in range(32):
        h0 = hq * 4
        pbase = 0 if h0 < 64 else 64
        war = [] if hq < 2 else [f"ye{hq - 2}"]
        psy = ps_y[hq % 2]

        @p.step(f"y{hq}", "tensor", deps=["scan", "toepd", "einit"] + war)
        def _(te, h0=h0, pbase=pbase, psy=psy):
            for k in range(4):
                h = h0 + k
                hh = h % 64
                xf_s = xf[:, h::128]
                tp_s = toep[:, h * 128:(h + 1) * 128]
                wr_s = wbr[pbase:pbase + 64, hh * 128:(hh + 1) * 128]
                wi_s = wbi[pbase:pbase + 64, hh * 128:(hh + 1) * 128]
                er_s = ere[pbase:pbase + 64, hh * 128:(hh + 1) * 128]
                ei_s = eim[pbase:pbase + 64, hh * 128:(hh + 1) * 128]
                out = psy[:, k * 128:(k + 1) * 128]
                te.matmul(out, xf_s, tp_s, start=True, stop=False)
                te.matmul(out, wr_s, er_s, start=False, stop=False)
                last = te.matmul(out, wi_s, ei_s, start=False, stop=True)
            return last

        gi = hq // 2
        wary = [] if gi < 2 else [f"ysend{gi - 2}", f"yhalo{gi - 2}"]

        @p.step(f"ye{hq}", "scalar", deps=[f"y{hq}"] + wary)
        def _(eng, hq=hq, psy=psy):
            buf = ybuf[(hq // 2) % 2]
            off = (hq % 2) * 512
            return eng.activation(buf[:, off:off + 512], psy[:], AF.Copy)

    for gi in range(16):
        ysq = ["sync", "gpsimd"][gi % 2]

        @p.dma(f"ysend{gi}", ysq, deps=[f"ye{2 * gi}", f"ye{2 * gi + 1}"])
        def _(eng, dma, gi=gi):
            h0 = gi * 8
            buf = ybuf[gi % 2]
            bufv = buf.rearrange("p (c e) -> p c e", c=8)
            for s in range(8):
                dst = bass.AP(dyb, s * 128 * 2049 + h0 * 2049 + 1,
                              [[128, 16], [2049, 8], [1, 128]])
                dma(dst, bufv[16 * s:16 * (s + 1)])

        @p.dma(f"yhalo{gi}", "sync", deps=[f"ye{2 * gi}", f"ye{2 * gi + 1}"])
        def _(eng, dma, gi=gi):
            h0 = gi * 8
            buf = ybuf[gi % 2]
            for s in range(1, 8):
                dst = bass.AP(dyb, s * 128 * 2049 + h0 * 2049,
                              [[1, 1], [2049, 8]])
                dma(dst, buf[16 * s - 1:16 * s, 127::128],
                    allow_slow_non_contiguous=True)
        dma_counts[f"ysend{gi}"] = 8
        dma_counts[f"yhalo{gi}"] = 7

    @p.dma("yzero", "sync", deps=["prep1"])
    def _(eng, dma):
        dma(bass.AP(dyb, 0, [[2049, 128], [1, 1]]), zcol[:, 0:1],
            allow_slow_non_contiguous=True)
    dma_counts["yzero"] = 1

    @p.step("a2a", "gpsimd",
            deps=["yzero"] + [f"ysend{gi}" for gi in range(16)]
            + [f"yhalo{gi}" for gi in range(16)])
    def _(g):
        return g.collective_compute(
            "AllToAll", ALU.bypass,
            replica_groups=[[0, 1, 2, 3, 4, 5, 6, 7]],
            ins=[io["dyb"].ap().opt()], outs=[io["ga"].ap().opt()])


def build_main(nc, ctx, p, io, dc, arena, ps):
    sb = lambda name, shape, dt_: ctx.enter_context(nc.sbuf_tensor(name, shape, dt_))

    def f32v(off, w):
        return arena[:, MAIN0 + off:MAIN0 + off + w]

    def bf16v(off, w_f32):
        return arena[:, MAIN0 + off:MAIN0 + off + w_f32].bitcast(BF16)

    g3 = lambda t: t.rearrange("p (g n) -> p g n", g=NG)
    xs_sl = [g3(f32v(0, 4096)), g3(f32v(4096, 4096))]
    ub = g3(f32v(8192, 4096))
    y1l = g3(f32v(12288, 4096))
    x1c = g3(f32v(22528, 4096))
    sgl = g3(f32v(16384, 4096))
    gb = g3(bf16v(20480, 2048))
    ycb = gb
    cb1 = g3(bf16v(22528, 2048))
    cb2 = g3(bf16v(24576, 2048))
    x1f = f32v(0, 4104).rearrange("p (g n) -> p g n", g=NG)
    xrb = g3(bf16v(4112, 2048))
    xkb = g3(bf16v(6160, 2048))
    kactb = g3(bf16v(8208, 2048))
    r2sq = g3(bf16v(8208, 2048))
    r_p = [f32v(10256, 512), f32v(10768, 512)]
    mixt = g3(f32v(11280, 4096))
    mixf = g3(f32v(15376, 4096))
    scr1 = mixf
    r2 = g3(f32v(19472, 4096))
    r2b = g3(bf16v(23568, 2048))
    x1r = [f32v(25616, 512), f32v(26128, 512)]
    wout_sb = arena[:, 0:8192].bitcast(BF16)
    wr_sb = arena[:, 8192:12288].bitcast(BF16)
    wk_sb = arena[:, 12288:16384].bitcast(BF16)
    wv_sb = arena[:, 16384:20480].bitcast(BF16)

    dvec = sb("dvec", [128, 8], F32)
    bvec = sb("bvec", [128, 16], F32)
    gvec = sb("gvec", [128, 8], F32)
    betv = sb("betv", [128, 8], F32)
    murv = sb("murv", [128, 8], F32)
    mukv = sb("mukv", [128, 8], F32)
    murm = sb("murm", [128, 8], F32)
    mukm = sb("mukm", [128, 8], F32)
    tcon = sb("tcon", [128, 8], F32)
    onesb = sb("onesb", [128, 128], BF16)
    onesf = sb("onesf", [1, 128], F32)
    bm = sb("bm", [128, 512], F32)
    bs = sb("bs", [128, 512], F32)
    t_spc = sb("t_spc", [1, 8], F32)

    nc.const_aps.register(F32, 1e-5, tcon[:, 2:3])

    xs_d = io["xs"]; ga_d = io["ga"]
    dx1 = io["dx1"]; dstats = io["dstats"]; xout = io["xout"]

    def SP(v):
        v.tensor_scalar(t_spc[:], t_spc[:], 1.0, None, ALU.mult)

    @p.dma("m_wload", "gpsimd", deps=["y31"])
    def _(eng, dma):
        for gq in range(NG):
            dma(wout_sb[:, gq * 2048:(gq + 1) * 2048],
                bass.AP(io["wout"], gq * 128 * 2048, [[2048, 128], [1, 2048]]))
    dc["m_wload"] = 8

    @p.dma("m_wload2", "gpsimd", deps=["y31"])
    def _(eng, dma):
        for wsb, wd in [(wr_sb, io["wr"]), (wk_sb, io["wk"]), (wv_sb, io["wv"])]:
            for gq in range(NG):
                dma(wsb[:, gq * 1024:(gq + 1) * 1024],
                    bass.AP(wd, gq * 128 * 1024, [[1024, 128], [1, 1024]]))
    dc["m_wload2"] = 24

    @p.dma("m_vload", "sync", deps=["a2a"])
    def _(eng, dma):
        slow = dict(allow_slow_non_contiguous=True)
        dma(dvec[:], bass.AP(io["dvecd"], 0, [[1, 128], [128, 8]]), **slow)
        dma(bvec[:], bass.AP(io["bvecd"], 0, [[1, 128], [128, 16]]), **slow)
        dma(gvec[:], bass.AP(io["gvecd"], 0, [[1, 128], [128, 8]]), **slow)
        dma(betv[:], bass.AP(io["betvd"], 0, [[1, 128], [128, 8]]), **slow)
        dma(murv[:], bass.AP(io["murd"], 0, [[1, 128], [128, 8]]), **slow)
        dma(mukv[:], bass.AP(io["mukd"], 0, [[1, 128], [128, 8]]), **slow)
        dma(tcon[:], io["consts"].ap())
        dma(onesb[:], io["onesd"].ap())
        dma(onesf[:], io["onesfd"].ap())
    dc["m_vload"] = 9

    @p.step("m_veci", "vector", deps=["m_vload"])
    def _(v):
        v.tensor_scalar(murm[:], murv[:], -1.0, 1.0, ALU.mult, ALU.add)
        v.memset(t_spc[:], 1.0)
        return v.tensor_scalar(mukm[:], mukv[:], -1.0, 1.0, ALU.mult, ALU.add)

    def stats(tag, srcb, srcsq, depc, bcdep, n, smdep=(), depc2=None):
        @p.step(f"sm{tag}a", "tensor", deps=[depc] + list(smdep) + bcdep)
        def _(te, n=n, srcb=srcb):
            for gq in range(NG):
                last = te.matmul(ps[4][:, 0:n], onesb[:], srcb[:, gq, 0:n],
                                 start=(gq == 0), stop=(gq == 7))
            return last

        @p.step(f"sm{tag}b", "tensor", deps=[depc2 or depc, f"sm{tag}a"])
        def _(te, n=n, srcsq=srcsq):
            for gq in range(NG):
                last = te.matmul(ps[5][:, 0:n], onesb[:], srcsq[:, gq, 0:n],
                                 start=(gq == 0), stop=(gq == 7))
            return last

        @p.step(f"pp{tag}v", "vector", deps=[f"sm{tag}a", f"sm{tag}b"])
        def _(v, n=n):
            zb = tcon[:, 3:4].broadcast_to([128, n])
            v.tensor_tensor(bm[:, 0:n], ps[4][:, 0:n], zb, ALU.add)
            v.tensor_tensor(bs[:, 0:n], ps[5][:, 0:n], zb, ALU.add)
            v.tensor_tensor(ps[4][:, 0:n], bm[:, 0:n], bm[:, 0:n], ALU.mult)
            SP(v)
            return v.tensor_tensor(bs[:, 0:n], bs[:, 0:n], ps[4][:, 0:n],
                                   ALU.subtract)

        @p.step(f"pp{tag}s", "scalar", deps=[f"pp{tag}v"])
        def _(eng, n=n):
            return eng.activation(bs[:, 0:n], bs[:, 0:n], AF.Sqrt,
                                  bias=tcon[:, 2:3])
        return f"pp{tag}s"

    def a_names(ci):
        cn = CHUNKS[ci][0]
        pv = CHUNKS[ci - 1][0] if ci >= 1 else None
        pv2 = CHUNKS[ci - 2][0] if ci >= 2 else None
        return cn, pv, pv2

    def attn_front(ci):
        cn, pv, pv2 = a_names(ci)
        C = cn
        off, n = CHUNKS[ci][1], CHUNKS[ci][2]
        xs_c = xs_sl[ci % 2]
        dxa_dep = ["m_veci", "y31"] + ([f"r1{pv2}"] if pv2 else [])
        dya_dep = ["a2a"] + ([f"mm{pv}_15"] if pv else [])

        @p.dma(f"dxa{C}", "sync", deps=dxa_dep)
        def _(eng, dma, off=off, n=n, xs_c=xs_c):
            dma(xs_c[:, :, 0:n],
                bass.AP(xs_d, off, [[TH, 128], [128 * TH, 8], [1, n]]),
                **(dict(allow_slow_non_contiguous=True) if n == 1 else {}))
        dc[f"dxa{C}"] = 1

        @p.dma(f"dya{C}", "gpsimd", deps=dya_dep)
        def _(eng, dma, off=off, n=n):
            dma(ycb[:, :, 0:n],
                bass.AP(ga_d, off, [[2049, 128], [128 * 2049, 8], [1, n]]),
                **(dict(allow_slow_non_contiguous=True) if n == 1 else {}))
        dc[f"dya{C}"] = 1

        @p.step(f"ud{C}", "scalar", deps=[f"dxa{C}"] + ([f"gel{pv}"] if pv else []))
        def _(eng, n=n, xs_c=xs_c):
            for gq in range(NG):
                last = eng.activation(ub[:, gq, 0:n], xs_c[:, gq, 0:n], AF.Copy,
                                      scale=dvec[:, gq:gq + 1])
            return last

        @p.step(f"uu{C}", "vector", deps=[f"ud{C}", f"dya{C}"])
        def _(v, n=n):
            for h in (slice(0, 4), slice(4, 8)):
                last = v.tensor_tensor(ub[:, h, 0:n], ub[:, h, 0:n],
                                       ycb[:, h, 0:n], ALU.add)
            return last

        gel_dep = [f"uu{C}"] + ([f"mm{pv}_15"] if pv else ["m_wload"])

        @p.step(f"gel{C}", "scalar", deps=gel_dep)
        def _(eng, n=n):
            return eng.activation(gb[:, :, 0:n], ub[:, :, 0:n], AF.Gelu_apprx_tanh)

    def attn_body(ci):
        cn, pv, pv2 = a_names(ci)
        C = cn
        off, n = CHUNKS[ci][1], CHUNKS[ci][2]
        xs_c = xs_sl[ci % 2]
        prev = pv

        BK6 = [0, 1, 2, 3, 6, 7]
        for m in range(16):
            bank = BK6[m % 6]
            if m >= 6:
                deps = [f"gel{C}", f"ev{C}_{m - 6}"]
            elif prev:
                pm = m + 12 if m < 4 else m + 6
                deps = [f"gel{C}", f"ev{prev}_{pm}"]
            else:
                deps = [f"gel{C}"]

            @p.step(f"mm{C}_{m}", "tensor", deps=deps)
            def _(te, m=m, bank=bank, n=n):
                for gq in range(NG):
                    last = te.matmul(
                        ps[bank][:, 0:n],
                        wout_sb[:, gq * 2048 + m * 128:gq * 2048 + (m + 1) * 128],
                        gb[:, gq, 0:n], start=(gq == 0), stop=(gq == 7))
                return last

            evdep = [f"mm{C}_{m}"]
            if prev:
                evdep += [f"gam{prev}"] if m < 8 else [f"x1{prev}"]
            if m < 8:
                @p.step(f"ev{C}_{m}", "scalar", deps=evdep)
                def _(eng, m=m, bank=bank, n=n):
                    return eng.activation(y1l[:, m, 0:n], ps[bank][:, 0:n],
                                          AF.Identity, bias=bvec[:, m:m + 1])
            else:
                @p.step(f"ev{C}_{m}", "scalar", deps=evdep)
                def _(eng, m=m, bank=bank, n=n):
                    return eng.activation(sgl[:, m - 8, 0:n], ps[bank][:, 0:n],
                                          AF.Sigmoid, bias=bvec[:, m:m + 1])

        @p.step(f"glu{C}", "vector", deps=[f"ev{C}_{14}", f"ev{C}_{15}",
                                           f"ev{C}_6", f"ev{C}_7"])
        def _(v, n=n):
            for h in (slice(0, 4), slice(4, 8)):
                last = v.tensor_tensor(y1l[:, h, 0:n], y1l[:, h, 0:n],
                                       sgl[:, h, 0:n], ALU.mult)
            return last

        cst_dep = [f"glu{C}"] + ([f"smo{prev}b", f"dxo{prev}", f"dxoa{prev}"] if prev else [])

        @p.step(f"cst{C}", "scalar", deps=cst_dep)
        def _(eng, n=n):
            return eng.activation(cb2[:, :, 0:n], y1l[:, :, 0:n], AF.Square)

        @p.step(f"cstv{C}", "vector", deps=cst_dep)
        def _(v, n=n):
            return v.tensor_scalar(cb1[:, :, 0:n], y1l[:, :, 0:n], 1.0, None, ALU.mult)

        sdone = stats(f"i{C}", cb1, cb2, f"cstv{C}", ([f"x1{pv}"] if pv else []), n,
                      smdep=([f"ppo{pv}v"] if pv else []), depc2=f"cst{C}")

        if ci + 1 < len(CHUNKS):
            attn_front(ci + 1)

        @p.step(f"api{C}", "vector", deps=[sdone])
        def _(v, n=n):
            bmb = lambda: bm[:, 0:n].unsqueeze(1).broadcast_to([128, 4, n])
            bsb = lambda: bs[:, 0:n].unsqueeze(1).broadcast_to([128, 4, n])
            v.reciprocal(bs[:, 0:n], bs[:, 0:n])
            for h in (slice(0, 4), slice(4, 8)):
                v.tensor_tensor(y1l[:, h, 0:n], y1l[:, h, 0:n], bmb(), ALU.subtract)
            for h in (slice(0, 4), slice(4, 8)):
                last = v.tensor_tensor(y1l[:, h, 0:n], y1l[:, h, 0:n], bsb(),
                                       ALU.mult)
            return last

        @p.step(f"gam{C}", "scalar", deps=[f"api{C}"])
        def _(eng, n=n):
            for gq in range(NG):
                last = eng.activation(sgl[:, gq, 0:n], y1l[:, gq, 0:n],
                                      AF.Identity, scale=gvec[:, gq:gq + 1],
                                      bias=betv[:, gq:gq + 1])
            return last

        @p.step(f"r1{C}", "vector", deps=[f"gam{C}"])
        def _(v, n=n, xs_c=xs_c):
            for h in (slice(0, 4), slice(4, 8)):
                last = v.tensor_tensor(sgl[:, h, 0:n], sgl[:, h, 0:n],
                                       xs_c[:, h, 0:n], ALU.add)
            return last

        @p.step(f"c1{C}", "scalar", deps=[f"r1{C}"])
        def _(eng, n=n):
            return eng.activation(cb2[:, :, 0:n], sgl[:, :, 0:n], AF.Square)

        @p.step(f"c1v{C}", "vector", deps=[f"r1{C}"])
        def _(v, n=n):
            return v.tensor_scalar(cb1[:, :, 0:n], sgl[:, :, 0:n], 1.0, None, ALU.mult)

        sdone1 = stats(f"o{C}", cb1, cb2, f"c1v{C}", [f"api{C}"], n,
                       smdep=[f"ppi{C}v"], depc2=f"c1{C}")

        @p.step(f"x1a{C}", "vector", deps=[sdone1])
        def _(v, n=n, is_halo=(ci == 0)):
            bmb = bm[:, 0:n].unsqueeze(1).broadcast_to([128, 4, n])
            bsb = bs[:, 0:n].unsqueeze(1).broadcast_to([128, 4, n])
            v.reciprocal(bs[:, 0:n], bs[:, 0:n])
            v.tensor_tensor(x1c[:, 0:4, 0:n], sgl[:, 0:4, 0:n], bmb, ALU.subtract)
            SP(v)
            last = v.tensor_tensor(x1c[:, 0:4, 0:n], x1c[:, 0:4, 0:n], bsb,
                                   ALU.mult)
            if is_halo:
                mb = tcon[:, 0:1].unsqueeze(1).broadcast_to([128, 4, n])
                SP(v)
                last = v.tensor_tensor(x1c[:, 0:4, 0:n], x1c[:, 0:4, 0:n], mb,
                                       ALU.mult)
            return last

        @p.dma(f"dxoa{C}", "sync", deps=[f"x1a{C}"])
        def _(eng, dma, off=off, n=n):
            dma(bass.AP(dx1, off, [[2049, 128], [128 * 2049, 4], [1, n]]),
                x1c[:, 0:4, 0:n],
                **(dict(allow_slow_non_contiguous=True) if n == 1 else {}))
        dc[f"dxoa{C}"] = 1

        @p.step(f"x1{C}", "vector", deps=[f"x1a{C}"])
        def _(v, n=n, is_halo=(ci == 0)):
            bmb = bm[:, 0:n].unsqueeze(1).broadcast_to([128, 4, n])
            bsb = bs[:, 0:n].unsqueeze(1).broadcast_to([128, 4, n])
            v.tensor_tensor(x1c[:, 4:8, 0:n], sgl[:, 4:8, 0:n], bmb, ALU.subtract)
            SP(v)
            last = v.tensor_tensor(x1c[:, 4:8, 0:n], x1c[:, 4:8, 0:n], bsb,
                                   ALU.mult)
            if is_halo:
                mb = tcon[:, 0:1].unsqueeze(1).broadcast_to([128, 4, n])
                SP(v)
                last = v.tensor_tensor(x1c[:, 4:8, 0:n], x1c[:, 4:8, 0:n], mb,
                                       ALU.mult)
            return last

        @p.dma(f"dxo{C}", "sync", deps=[f"x1{C}"])
        def _(eng, dma, off=off, n=n):
            dma(bass.AP(dx1, 4 * 128 * 2049 + off, [[2049, 128], [128 * 2049, 4], [1, n]]),
                x1c[:, 4:8, 0:n],
                **(dict(allow_slow_non_contiguous=True) if n == 1 else {}))
        dc[f"dxo{C}"] = 1

    attn_front(0)
    for ci in range(len(CHUNKS)):
        attn_body(ci)

    def ffn_front(ci):
        cn, off, n = FCHUNKS[ci]
        C = "f" + cn
        pf = f"f{FCHUNKS[ci - 1][0]}" if ci else None
        dxf_extra = {0: ["dxoh", "dxoah", "dxo0", "dxoa0", "r13", "r12"],
                     1: ["dxo1", "dxoa1"], 2: ["dxo2", "dxoa2"],
                     3: ["dxo3", "dxoa3"]}[ci]

        @p.dma(f"dxf{C}", "sync", deps=dxf_extra + ([f"vxk{pf}"] if pf else []))
        def _(eng, dma, off=off, n=n):
            dma(x1f[:, :, 0:n + 1],
                bass.AP(dx1, off, [[2049, 128], [128 * 2049, 8], [1, n + 1]]))
        dc[f"dxf{C}"] = 1

        @p.step(f"mxr{C}", "scalar", deps=[f"dxf{C}"] +
                ([f"kact{pf}"] if pf else ["x13", "dxo3", "dxoa3"]))
        def _(eng, n=n):
            for gq in range(NG):
                eng.activation(mixf[:, gq, 0:n], x1f[:, gq, 1:n + 1], AF.Copy,
                               scale=murv[:, gq:gq + 1])
            for gq in range(NG):
                last = eng.activation(mixt[:, gq, 0:n], x1f[:, gq, 0:n], AF.Copy,
                                      scale=murm[:, gq:gq + 1])
            return last

        @p.step(f"vxr{C}", "vector", deps=[f"mxr{C}"] +
                ([f"rm{pf}_7"] if pf else []))
        def _(v, n=n):
            for h in (slice(0, 4), slice(4, 8)):
                last = v.tensor_tensor(xrb[:, h, 0:n], mixf[:, h, 0:n],
                                       mixt[:, h, 0:n], ALU.add)
            return last

        @p.step(f"mxk{C}", "scalar", deps=[f"vxr{C}"])
        def _(eng, n=n):
            for gq in range(NG):
                eng.activation(mixf[:, gq, 0:n], x1f[:, gq, 1:n + 1], AF.Copy,
                               scale=mukv[:, gq:gq + 1])
            for gq in range(NG):
                last = eng.activation(mixt[:, gq, 0:n], x1f[:, gq, 0:n], AF.Copy,
                                      scale=mukm[:, gq:gq + 1])
            return last

        @p.step(f"vxk{C}", "vector", deps=[f"mxk{C}"] +
                ([f"km{pf}_7"] if pf else []))
        def _(v, n=n):
            for h in (slice(0, 4), slice(4, 8)):
                last = v.tensor_tensor(xkb[:, h, 0:n], mixf[:, h, 0:n],
                                       mixt[:, h, 0:n], ALU.add)
            return last

    def ffn_body(ci):
        cn, off, n = FCHUNKS[ci]
        C = "f" + cn
        pf = f"f{FCHUNKS[ci - 1][0]}" if ci else None

        BK4 = [0, 1, 6, 7]
        for m in range(NG):
            bank = BK4[m % 4]
            deps = [f"vxk{C}", "m_wload2"] + ([f"kev{C}_{m - 4}"] if m >= 4
                    else ([f"kev{pf}_{m + 4}"] if pf else []))

            @p.step(f"km{C}_{m}", "tensor", deps=deps)
            def _(te, m=m, bank=bank, n=n):
                for gq in range(NG):
                    last = te.matmul(
                        ps[bank][:, 0:n],
                        wk_sb[:, gq * 1024 + m * 128:gq * 1024 + (m + 1) * 128],
                        xkb[:, gq, 0:n], start=(gq == 0), stop=(gq == 7))
                return last

            @p.step(f"kev{C}_{m}", "scalar", deps=[f"km{C}_{m}", f"vxk{C}"])
            def _(eng, m=m, bank=bank, n=n):
                return eng.activation(scr1[:, m, 0:n], ps[bank][:, 0:n], AF.Relu)

        @p.step(f"kact{C}", "scalar", deps=[f"kev{C}_7"] +
                ([f"smt{pf}b"] if pf else []))
        def _(eng, n=n):
            return eng.activation(kactb[:, :, 0:n], scr1[:, :, 0:n], AF.Square)

        for m in range(NG):
            @p.step(f"rm{C}_{m}", "tensor",
                    deps=[f"vxr{C}", "m_wload2"] + ([f"rev{C}_{m - 1}"] if m
                         else ([f"rev{pf}_7"] if pf else [])))
            def _(te, m=m, n=n):
                for gq in range(NG):
                    last = te.matmul(
                        ps[2][:, 0:n],
                        wr_sb[:, gq * 1024 + m * 128:gq * 1024 + (m + 1) * 128],
                        xrb[:, gq, 0:n], start=(gq == 0), stop=(gq == 7))
                return last

            @p.step(f"rev{C}_{m}", "scalar", deps=[f"rm{C}_{m}"] +
                    ([f"gt{C}_{m - 2}"] if m >= 2 else
                     ([f"gt{pf}_{m + 6}"] if pf else [])))
            def _(eng, m=m, n=n):
                return eng.activation(r_p[m % 2][:, 0:n], ps[2][:, 0:n], AF.Sigmoid)

            @p.step(f"vm{C}_{m}", "tensor",
                    deps=[f"kact{C}"] + ([f"gt{C}_{m - 1}"] if m
                         else ([f"gt{pf}_7"] if pf else [])))
            def _(te, m=m, n=n):
                for gq in range(NG):
                    last = te.matmul(
                        ps[3][:, 0:n],
                        wv_sb[:, gq * 1024 + m * 128:gq * 1024 + (m + 1) * 128],
                        kactb[:, gq, 0:n], start=(gq == 0), stop=(gq == 7))
                return last

            @p.dma(f"xld{C}_{m}", "gpsimd",
                   deps=[f"dxo{cn}", f"dxoa{cn}"]
                   + ([f"resm{C}_{m - 2}"] if m >= 2 else
                      ([f"resm{pf}_{m + 6}"] if pf else ["dxo3", "dxoa3"])))
            def _(eng, dma, m=m, off=off, n=n):
                dma(x1r[m % 2][:, 0:n],
                    bass.AP(dx1, off + 1 + m * 128 * 2049,
                            [[2049, 128], [1, n]]))
            dc[f"xld{C}_{m}"] = 1

            @p.step(f"gt{C}_{m}", "vector", deps=[f"vm{C}_{m}", f"rev{C}_{m}"] +
                    ([f"dout{pf}"] if pf and m == 0 else []))
            def _(v, m=m, n=n):
                return v.tensor_tensor(r2[:, m, 0:n], ps[3][:, 0:n],
                                       r_p[m % 2][:, 0:n], ALU.mult)

            @p.step(f"resm{C}_{m}", "vector", deps=[f"gt{C}_{m}", f"xld{C}_{m}"])
            def _(v, m=m, n=n):
                return v.tensor_tensor(r2[:, m, 0:n], r2[:, m, 0:n],
                                       x1r[m % 2][:, 0:n], ALU.add)

        if ci + 1 < len(FCHUNKS):
            ffn_front(ci + 1)

        @p.step(f"c2{C}", "scalar", deps=[f"resm{C}_7"] +
                ([f"smt{pf}b"] if pf else []))
        def _(eng, n=n):
            return eng.activation(r2sq[:, :, 0:n], r2[:, :, 0:n], AF.Square)

        @p.step(f"c2v{C}", "vector", deps=[f"resm{C}_7"] +
                ([f"smt{pf}b"] if pf else []))
        def _(v, n=n):
            return v.tensor_scalar(r2b[:, :, 0:n], r2[:, :, 0:n], 1.0, None, ALU.mult)

        sdone2 = stats(f"t{C}", r2b, r2sq, f"c2v{C}",
                       ([f"ap2f{FCHUNKS[ci - 1][0]}"] if ci else ["x1" + CHUNKS[-1][0]]), n,
                       smdep=([f"pptf{FCHUNKS[ci - 1][0]}v"] if ci else [f"ppo{CHUNKS[-1][0]}v"]),
                       depc2=f"c2{C}")

        @p.step(f"ap2{C}", "vector", deps=[sdone2])
        def _(v, n=n):
            bmb = lambda: bm[:, 0:n].unsqueeze(1).broadcast_to([128, 4, n])
            bsb = lambda: bs[:, 0:n].unsqueeze(1).broadcast_to([128, 4, n])
            v.reciprocal(bs[:, 0:n], bs[:, 0:n])
            for h in (slice(0, 4), slice(4, 8)):
                v.tensor_tensor(r2[:, h, 0:n], r2[:, h, 0:n], bmb(), ALU.subtract)
            for h in (slice(0, 4), slice(4, 8)):
                last = v.tensor_tensor(r2[:, h, 0:n], r2[:, h, 0:n],
                                       bsb(), ALU.mult)
            return last

        @p.dma(f"dout{C}", "sync", deps=[f"ap2{C}"])
        def _(eng, dma, off=off, n=n):
            dma(bass.AP(xout, off, [[T, 128], [128 * T, 8], [1, n]]),
                r2[:, :, 0:n])
        dc[f"dout{C}"] = 1

    ffn_front(0)
    for ci in range(len(FCHUNKS)):
        ffn_body(ci)
    prev_last = "doutf3"

    return [f"doutf{cn}" for cn, _, _ in FCHUNKS]


_CACHE = {}


def _build():
    if "prog" in _CACHE:
        return _CACHE["prog"]
    nc = bass.Bass("TRN2", target_bir_lowering=False)
    io = {}
    io["xc"] = nc.dram_tensor("xc", [NTOK * 128], F32, kind="ExternalInput")
    io["ccol"] = nc.dram_tensor("ccol", [128, 8], F32, kind="ExternalInput")
    io["are"] = nc.dram_tensor("are", [H, N], F32, kind="ExternalInput")
    io["aim"] = nc.dram_tensor("aim", [H, N], F32, kind="ExternalInput")
    io["ldt"] = nc.dram_tensor("ldt", [H, 1], F32, kind="ExternalInput")
    io["cre"] = nc.dram_tensor("cre", [H, N], F32, kind="ExternalInput")
    io["cim"] = nc.dram_tensor("cim", [H, N], F32, kind="ExternalInput")
    io["anti"] = nc.dram_tensor("anti", [128, 128], BF16, kind="ExternalInput")
    io["ident"] = nc.dram_tensor("ident", [128, 128], F32, kind="ExternalInput")
    io["iota_t"] = nc.dram_tensor("iota_t", [128, 2048], F32, kind="ExternalInput")
    io["xs"] = nc.dram_tensor("xs", [1024, TH], F32, kind="ExternalInput")
    io["wout"] = nc.dram_tensor("wout", [1024, 2048], F32, kind="ExternalInput")
    io["wr"] = nc.dram_tensor("wr", [1024, 1024], F32, kind="ExternalInput")
    io["wk"] = nc.dram_tensor("wk", [1024, 1024], F32, kind="ExternalInput")
    io["wv"] = nc.dram_tensor("wv", [1024, 1024], F32, kind="ExternalInput")
    io["dvecd"] = nc.dram_tensor("dvecd", [1024], F32, kind="ExternalInput")
    io["bvecd"] = nc.dram_tensor("bvecd", [2048], F32, kind="ExternalInput")
    io["gvecd"] = nc.dram_tensor("gvecd", [1024], F32, kind="ExternalInput")
    io["betvd"] = nc.dram_tensor("betvd", [1024], F32, kind="ExternalInput")
    io["murd"] = nc.dram_tensor("murd", [1024], F32, kind="ExternalInput")
    io["mukd"] = nc.dram_tensor("mukd", [1024], F32, kind="ExternalInput")
    io["consts"] = nc.dram_tensor("consts", [128, 8], F32, kind="ExternalInput")
    io["onesd"] = nc.dram_tensor("onesd", [128, 128], BF16, kind="ExternalInput")
    io["onesfd"] = nc.dram_tensor("onesfd", [1, 128], F32, kind="ExternalInput")
    io["dpar"] = nc.dram_tensor("dpar", [4 * 8192], F32)
    io["dkpad"] = nc.dram_tensor("dkpad", [128 * 256], F32)
    io["dlq"] = nc.dram_tensor("dlq", [128 * 128], F32)
    io["dyb"] = nc.dram_tensor("dyb", [8 * 128 * 2049], BF16)
    io["ga"] = nc.dram_tensor("ga", [8 * 128 * 2049], BF16)
    io["dx1"] = nc.dram_tensor("dx1", [8 * 128 * TH], F32)
    io["dstats"] = nc.dram_tensor("dstats", [32 * 1024], F32)
    io["xout"] = nc.dram_tensor("xout", [1024, T], F32, kind="ExternalOutput")

    ctx = contextlib.ExitStack()
    with ctx:
        p = Prog(nc, ctx)
        dc = {}
        arena = ctx.enter_context(nc.sbuf_tensor("arena", [128, AR_W], F32))
        ps = [ctx.enter_context(nc.psum_tensor(f"ps{i}", [128, 512], F32))
              for i in range(8)]
        build_conv(nc, ctx, p, io, dc, arena, ps)
        douts = build_main(nc, ctx, p, io, dc, arena, ps)

        @p.step("fin", "gpsimd", deps=douts)
        def _(eng):
            return eng.engine_nop()

        block = ctx.enter_context(nc.Block())
        p.build(block, dc)
    _CACHE["prog"] = nc
    return nc


def kernel(L0_attn_x, L0_attn_A_re, L0_attn_A_im, L0_attn_log_dt, L0_attn_C_re,
           L0_attn_C_im, L0_attn_D, L0_attn_W_out, L0_attn_b_out, L0_attn_gamma,
           L0_attn_beta, L0_ffn_W_r, L0_ffn_W_k, L0_ffn_W_v, L0_ffn_mu_r,
           L0_ffn_mu_k):
    f32 = lambda a: np.ascontiguousarray(np.asarray(a, dtype=np.float32))
    x = f32(L0_attn_x)
    A_re = f32(L0_attn_A_re); A_im = f32(L0_attn_A_im)
    log_dt = f32(L0_attn_log_dt)
    C_re = f32(L0_attn_C_re); C_im = f32(L0_attn_C_im)
    D = f32(L0_attn_D)
    W_out = f32(L0_attn_W_out); b_out = f32(L0_attn_b_out)
    gamma = f32(L0_attn_gamma); beta = f32(L0_attn_beta)
    W_rT = np.ascontiguousarray(f32(L0_ffn_W_r).T)
    W_kT = np.ascontiguousarray(f32(L0_ffn_W_k).T)
    W_vT = np.ascontiguousarray(f32(L0_ffn_W_v).T)
    mu_r = f32(L0_ffn_mu_r); mu_k = f32(L0_ffn_mu_k)

    nc = _build()
    in_maps = []
    ones_bf = np.full((128, 128), 1.0 / 1024, ml_dtypes.bfloat16)
    for s in range(8):
        r = s
        ch = slice(128 * r, 128 * (r + 1))
        b, w = s // 4, s % 4
        t0 = w * 2048
        xs_np = np.zeros((1024, TH), np.float32)
        xs_np[:, 1:] = x[b, t0:t0 + 2048].T
        if w > 0:
            xs_np[:, 0] = x[b, t0 - 1]
        cons = np.zeros((128, 8), np.float32)
        cons[:, 0] = 1.0 if w > 0 else 0.0
        cons[:, 1] = 1.0
        cons[:, 2] = 1e-5
        cons[:, 3] = 0.0
        in_maps.append(dict(
            xc=np.ascontiguousarray(x[:, :, ch]).reshape(-1),
            ccol=CCOL_NP, are=A_re[ch], aim=A_im[ch],
            ldt=log_dt[ch][:, None], cre=C_re[ch], cim=C_im[ch],
            anti=ANTI_NP, ident=IDENT_NP, iota_t=IOTA_NP,
            xs=xs_np, wout=W_out, wr=W_rT, wk=W_kT, wv=W_vT,
            dvecd=D, bvecd=b_out, gvecd=gamma, betvd=beta,
            murd=mu_r, mukd=mu_k, consts=cons, onesd=ones_bf,
            onesfd=np.ones((1, 128), np.float32)))

    res = run_bass_kernel_spmd(nc, in_maps, core_ids=list(range(8)))
    out = np.empty((2, 8192, 1024), np.float32)
    for s in range(8):
        b, w = s // 4, s % 4
        t0 = w * 2048
        out[b, t0:t0 + 2048] = res.results[s]["xout"].T
    return out


# version 52
# speedup vs baseline: 1.1328x; 1.1328x over previous
import os
os.environ.setdefault("TRN_ALLOW_ICI_ALL_TO_ALL", "1")
import contextlib
import numpy as np
import ml_dtypes
import concourse.bass as bass
import concourse.mybir as mybir
from concourse.bass_utils import run_bass_kernel_spmd

F32 = mybir.dt.float32
BF16 = mybir.dt.bfloat16
I32 = mybir.dt.int32
AF = mybir.ActivationFunctionType
ALU = mybir.AluOpType
AX = mybir.AxisListType

class _Step:
    __slots__ = ("name", "engine", "deps", "emit", "is_dma", "ndma",
                 "seq", "dma_hi", "marked", "idx")

    def __init__(self, name, engine, deps, emit, is_dma):
        self.name = name
        self.engine = engine
        self.deps = list(deps)
        self.emit = emit
        self.is_dma = is_dma
        self.ndma = 0
        self.seq = 0
        self.dma_hi = 0
        self.marked = False
        self.idx = 0


class Prog:
    ENGINES = ("sync", "gpsimd", "scalar", "vector", "tensor")

    def __init__(self, nc, ctx):
        self.nc = nc
        self.steps = []
        self.by_name = {}
        self.step_sem = {e: ctx.enter_context(nc.semaphore(f"ss_{e}"))
                         for e in self.ENGINES}
        self.dma_sem = {e: ctx.enter_context(nc.semaphore(f"sd_{e}"))
                        for e in ("sync", "gpsimd", "scalar", "vector")}

    def step(self, name, engine, deps=()):
        def reg(fn):
            s = _Step(name, engine, deps, fn, False)
            assert name not in self.by_name, name
            self.by_name[name] = s
            self.steps.append(s)
            return name
        return reg

    def dma(self, name, engine, deps=()):
        def reg(fn):
            s = _Step(name, engine, deps, fn, True)
            assert name not in self.by_name, name
            self.by_name[name] = s
            self.steps.append(s)
            return name
        return reg

    def build(self, block, dma_counts):
        for s in self.steps:
            for d in s.deps:
                ds = self.by_name[d]
                if ds.engine != s.engine and not ds.is_dma:
                    ds.marked = True
        cnt = {e: 0 for e in self.ENGINES}
        dcnt = {e: 0 for e in self.ENGINES}
        for i, s in enumerate(self.steps):
            s.idx = i
            if s.is_dma:
                s.ndma = dma_counts[s.name]
                dcnt[s.engine] += s.ndma
                s.dma_hi = dcnt[s.engine]
            else:
                if s.marked:
                    cnt[s.engine] += 1
                    s.seq = cnt[s.engine]
        prog = self

        def engine_fn(engine):
            def run(eng):
                waited = {}

                def need(sem, val):
                    key = id(sem)
                    if waited.get(key, -1) < val:
                        eng.wait_ge(sem, val)
                        waited[key] = val

                ndma_issued = [0]

                def dma(out, in_, **kw):
                    ndma_issued[0] += 1
                    return eng.dma_start(out=out, in_=in_, **kw).then_inc(
                        prog.dma_sem[engine], 16)

                for s in prog.steps:
                    if s.engine != engine:
                        continue
                    for d in s.deps:
                        ds = prog.by_name[d]
                        if ds.is_dma:
                            need(prog.dma_sem[ds.engine], 16 * ds.dma_hi)
                        elif ds.engine != engine:
                            need(prog.step_sem[ds.engine], ds.seq)
                    before = ndma_issued[0]
                    if s.is_dma:
                        s.emit(eng, dma)
                        assert ndma_issued[0] - before == s.ndma, (
                            f"{s.name}: declared {s.ndma}, issued "
                            f"{ndma_issued[0] - before}")
                    else:
                        last = s.emit(eng)
                        if s.marked:
                            if last is None:
                                eng.engine_nop().then_inc(prog.step_sem[engine], 1)
                            else:
                                last.then_inc(prog.step_sem[engine], 1)
            return run

        for e in self.ENGINES:
            getattr(block, e)(engine_fn(e))


H, N, Q, J, L = 128, 64, 128, 128, 8192
NTOK = 2 * L
INV2PI = float(1.0 / (2 * np.pi))
TWOPI = float(2 * np.pi)
NQT = 4
QW = H * N // NQT
NG = 8
T = 2048
TH = T + 1
CHUNKS = [("h", 0, 1), ("0", 1, 512), ("1", 513, 512), ("2", 1025, 512),
          ("3", 1537, 512)]
FCHUNKS = [("0", 0, 512), ("1", 512, 512), ("2", 1024, 512), ("3", 1536, 512)]

CCOL_NP = np.zeros((128, 8), dtype=np.float32)
CCOL_NP[:, 0] = np.arange(128) + 1
CCOL_NP[:, 1] = np.pi / 2
CCOL_NP[:, 3] = 1.0
ANTI_NP = np.eye(128, dtype=ml_dtypes.bfloat16)[::-1].copy()
IDENT_NP = np.eye(128, dtype=np.float32)
IOTA_NP = np.broadcast_to((np.arange(128, dtype=np.float32) + 1)[:, None],
                          (128, 2048)).copy()

AR_W = 47120
MAIN0 = 20480


def build_conv(nc, ctx, p, io, dma_counts, arena, ps):
    sb = lambda name, shape, dt_: ctx.enter_context(nc.sbuf_tensor(name, shape, dt_))

    def f32v(off, w):
        return arena[:, off:off + w]

    def bf16v(off, w_f32):
        return arena[:, off:off + w_f32].bitcast(BF16)

    ere = bf16v(0, 4096)
    eim = bf16v(4096, 4096)
    p2re = bf16v(8192, 4096)
    p2im = bf16v(12288, 4096)
    wbr = p2re
    wbi = p2im
    GEN0 = 16384
    g_rep = f32v(GEN0 + 0 * QW, QW)
    g_arg = f32v(GEN0 + 1 * QW, QW)
    g_mag = f32v(GEN0 + 2 * QW, QW)
    g_t1 = f32v(GEN0 + 3 * QW, QW)
    g_t2 = f32v(GEN0 + 4 * QW, QW)
    g_t3 = f32v(GEN0 + 5 * QW, QW)
    g_i = f32v(GEN0 + 6 * QW, QW).bitcast(I32)
    g_tre = f32v(GEN0 + 7 * QW, QW)
    g_tim = f32v(GEN0 + 8 * QW, QW)
    xf = bf16v(GEN0, 8192)
    toep = bf16v(GEN0 + 8192, 8192)
    XT0 = GEN0 + 18432
    xt = bf16v(XT0, 8192)
    ure = bf16v(XT0, 4096)
    uim = bf16v(XT0 + 4096, 4096)
    t_iota = f32v(43008, 2048)
    u_stage = [f32v(45056, 1024), f32v(46080, 1024)]
    ybuf = [bf16v(38912, 512), bf16v(39424, 512)]

    t_ccol = sb("t_ccol", [128, 8], F32)
    t_are = sb("t_are", [H, N], F32)
    t_aim = sb("t_aim", [H, N], F32)
    t_ldt = sb("t_ldt", [H, 1], F32)
    t_cre = sb("t_cre", [H, N], F32)
    t_cim = sb("t_cim", [H, N], F32)
    t_anti = sb("t_anti", [128, 128], BF16)
    t_id = sb("t_id", [128, 128], F32)
    t_wq = sb("t_wq", [128, 12 * 64], F32)
    t_w = [t_wq[:, i * 64:(i + 1) * 64] for i in range(12)]
    t_wi = sb("t_wi", [H, N], I32)
    t_wi2 = sb("t_wi2", [H, N], I32)
    t_dmy = sb("t_dmy", [H, N], F32)
    t_dt = sb("t_dt", [H, 1], F32)
    t_k0 = sb("t_k0", [H, 1], F32)
    krow = sb("krow", [128, 256], F32)
    ksub = sb("ksub", [128, 128], F32)
    lqr_t = sb("lqr_t", [128, 128], F32)
    lqi_t = sb("lqi_t", [128, 128], F32)
    lqr = lqr_t[:, 0:64]
    lqi = lqi_t[:, 0:64]
    s_q = f32v(43008, 512)
    s_ = [s_q[:, i * 128:(i + 1) * 128] for i in range(4)]
    s3 = [t.rearrange("p (a b) -> p a b", a=64) for t in s_]
    wst_q = f32v(43520, 768)
    wst = [wst_q[:, i * 128:(i + 1) * 128] for i in range(6)]
    t_lam2 = sb("t_lam2", [128, 128], F32)
    t_lam3 = sb("t_lam3", [128, 128], F32)
    t_lqs = sb("t_lqs", [128, 128], F32)
    t_lqs2 = sb("t_lqs2", [128, 128], F32)
    zcol = sb("zcol", [128, 2], BF16)

    ps_a, ps_b = ps[0], ps[1]
    ps_u = ps[2:6]
    ps_y = ps[6:8]

    nc.const_aps.register(F32, 0.0, t_ccol[:, 2:3])
    nc.const_aps.register(F32, 1.0, t_ccol[:, 3:4])

    io_iota = t_iota
    ccol = io["ccol"]; are = io["are"]; aim = io["aim"]; ldt = io["ldt"]
    cre = io["cre"]; cim = io["cim"]; anti = io["anti"]; ident = io["ident"]
    xc = io["xc"]; dpar = io["dpar"]; dkpad = io["dkpad"]; dyb = io["dyb"]
    dlq = io["dlq"]

    ure_r = ure.rearrange("p (a b) -> p a b", a=64)
    uim_r = uim.rearrange("p (a b) -> p a b", a=64)
    wbr_r = wbr.rearrange("p (a b) -> p a b", a=64)
    wbi_r = wbi.rearrange("p (a b) -> p a b", a=64)
    lqr_b = lqr.unsqueeze(2).broadcast_to([128, 64, 2])
    lqi_b = lqi.unsqueeze(2).broadcast_to([128, 64, 2])
    wst3 = [t.rearrange("p (a b) -> p a b", a=64) for t in wst]
    wst3_r = [wst3[0], wst3[2], wst3[4]]
    wst3_i = [wst3[1], wst3[3], wst3[5]]

    @p.dma("ld_par", "sync")
    def _(eng, dma):
        for t_, d_ in [(t_ccol, ccol), (t_are, are), (t_aim, aim), (t_ldt, ldt),
                       (t_cre, cre), (t_cim, cim), (t_id, ident), (t_anti, anti)]:
            dma(t_[:], d_.ap())
        dma(t_iota, io["iota_t"].ap())
    dma_counts["ld_par"] = 9

    @p.dma("ld_x", "gpsimd", deps=[])
    def _(eng, dma):
        for jr in range(8):
            src = bass.AP(xc, (jr * 16) * 128 * 128,
                          [[128, 128], [128 * 128, 16], [1, 128]])
            dst = xt[:, jr * 16 * 128:(jr + 1) * 16 * 128]
            dma(dst.rearrange("p (a b) -> p a b", a=16), src)
    dma_counts["ld_x"] = 8

    @p.step("sc_dt", "scalar", deps=["ld_par"])
    def _(eng):
        return eng.activation(t_dt[:], t_ldt[:], AF.Exp)

    def round_reduce_pair(v, src, s1, o1, s2, o2, tfa, tia, tfb, tib):
        TS = v.tensor_scalar
        TS(tfa[:], src[:], INV2PI, s1, ALU.mult, ALU.add)
        TS(tfb[:], src[:], INV2PI, s2, ALU.mult, ALU.add)
        v.tensor_copy(tia[:], tfa[:])
        v.tensor_copy(tib[:], tfb[:])
        v.tensor_copy(tfa[:], tia[:])
        v.tensor_copy(tfb[:], tib[:])
        v.scalar_tensor_tensor(o1[:], tfa[:], -TWOPI, src[:], ALU.mult, ALU.add)
        return v.scalar_tensor_tensor(o2[:], tfb[:], -TWOPI, src[:], ALU.mult, ALU.add)

    @p.step("prep1", "vector", deps=["ld_par", "sc_dt"])
    def _(v):
        TT, TS = v.tensor_tensor, v.tensor_scalar
        v.tensor_copy(t_dmy[:], t_dt[:].broadcast_to([H, N]))
        TS(t_w[0][:], t_are[:], t_dt[:], None, ALU.mult)
        TT(t_w[0][:], t_are[:], t_dmy[:], ALU.mult)
        TT(t_w[1][:], t_aim[:], t_dmy[:], ALU.mult)
        v.memset(zcol[:], 0.0)
        return round_reduce_pair(v, t_w[1], 0.0, t_w[6], 0.25, t_w[7],
                                 t_w[10], t_wi, t_w[11], t_wi2)

    @p.step("sc_trig1", "scalar", deps=["prep1"])
    def _(eng):
        eng.activation(t_w[4][:], t_w[0][:], AF.Exp)
        eng.activation(t_w[5][:], t_w[6][:], AF.Sin)
        return eng.activation(t_w[6][:], t_w[7][:], AF.Sin, bias=t_ccol[:, 1:2])

    @p.step("prep2", "vector", deps=["sc_trig1"])
    def _(v):
        TT, TS = v.tensor_tensor, v.tensor_scalar

        def SP():
            v.tensor_copy(t_dmy[:], t_w[0][:])
        TT(t_w[8][:], t_w[4][:], t_w[6][:], ALU.mult)
        TT(t_w[9][:], t_w[4][:], t_w[5][:], ALU.mult)
        TT(t_w[4][:], t_are[:], t_are[:], ALU.mult)
        TT(t_w[11][:], t_aim[:], t_aim[:], ALU.mult)
        TS(t_w[10][:], t_w[8][:], 1.0, None, ALU.subtract)
        TT(t_w[4][:], t_w[4][:], t_w[11][:], ALU.add)
        TT(t_w[6][:], t_w[10][:], t_are[:], ALU.mult)
        TT(t_w[7][:], t_w[9][:], t_aim[:], ALU.mult)
        v.reciprocal(t_w[5][:], t_w[4][:])
        TT(t_w[6][:], t_w[6][:], t_w[7][:], ALU.add)
        TT(t_w[8][:], t_w[9][:], t_are[:], ALU.mult)
        TT(t_w[7][:], t_w[10][:], t_aim[:], ALU.mult)
        TT(t_w[6][:], t_w[6][:], t_w[5][:], ALU.mult)
        TT(t_w[8][:], t_w[8][:], t_w[7][:], ALU.subtract)
        TT(t_w[10][:], t_cre[:], t_w[6][:], ALU.mult)
        TT(t_w[8][:], t_w[8][:], t_w[5][:], ALU.mult)
        TT(t_w[7][:], t_cre[:], t_w[8][:], ALU.mult)
        SP()
        TT(t_w[11][:], t_cim[:], t_w[8][:], ALU.mult)
        TT(t_w[9][:], t_cim[:], t_w[6][:], ALU.mult)
        TT(t_w[2][:], t_w[10][:], t_w[11][:], ALU.subtract)
        TT(t_w[3][:], t_w[7][:], t_w[9][:], ALU.add)
        TS(t_w[2][:], t_w[2][:], 2.0, None, ALU.mult)
        TS(t_w[3][:], t_w[3][:], 2.0, None, ALU.mult)
        SP()
        v.reduce_sum(t_k0[:], t_w[2][:], AX.X)
        TS(t_w[7][:], t_w[0][:], 128.0, None, ALU.mult)
        TS(t_w[4][:], t_w[1][:], 128.0, None, ALU.mult)
        SP()
        return round_reduce_pair(v, t_w[4], 0.0, t_w[8], 0.25, t_w[9],
                                 t_w[10], t_wi, t_w[11], t_wi2)

    @p.step("sc_trig2", "scalar", deps=["prep2"])
    def _(eng):
        eng.activation(t_w[7][:], t_w[7][:], AF.Exp)
        eng.activation(t_w[8][:], t_w[8][:], AF.Sin)
        return eng.activation(t_w[9][:], t_w[9][:], AF.Sin, bias=t_ccol[:, 1:2])

    @p.step("c_ones_i", "vector", deps=["ld_par"])
    def _(v):
        return v.memset(krow[0:1, 0:128], 1.0)

    @p.step("prep3", "vector", deps=["sc_trig2"])
    def _(v):
        v.tensor_tensor(t_w[4][:], t_w[7][:], t_w[9][:], ALU.mult)
        v.tensor_tensor(t_w[5][:], t_w[7][:], t_w[8][:], ALU.mult)
        v.memset(t_lam2[0:64, 64:128], 0.0)
        v.memset(t_lam2[64:128, 0:64], 0.0)
        v.memset(t_lam3[0:64, 64:128], 0.0)
        v.memset(t_lam3[64:128, 0:64], 0.0)
        v.tensor_copy(t_lam2[64:128, 64:128], t_w[4][64:128, :])
        v.tensor_copy(t_lam3[64:128, 64:128], t_w[5][64:128, :])
        v.tensor_copy(t_lam3[0:64, 0:64], t_w[5][0:64, :])
        return v.tensor_copy(t_lam2[0:64, 0:64], t_w[4][0:64, :])

    @p.dma("bounce", "sync", deps=["prep3"])
    def _(eng, dma):
        for i, t_ in enumerate([t_w[0], t_w[1], t_w[2], t_w[3]]):
            dma(bass.AP(dpar, i * 8192, [[64, 128], [1, 64]]), t_[:])
    dma_counts["bounce"] = 4

    @p.step("t_lamq", "tensor", deps=["prep3", "ld_par"])
    def _(te):
        te.transpose(ps_a[:, 0:128], t_lam2[:], t_id[:])
        return te.transpose(ps_b[:, 0:128], t_lam3[:], t_id[:])

    @p.step("v_lamq0", "vector", deps=["t_lamq"])
    def _(v):
        v.tensor_copy(t_lqs[:], ps_a[:, 0:128])
        v.tensor_copy(t_lqs2[:], ps_b[:, 0:128])
        v.tensor_copy(lqr_t[0:64, 0:64], t_lqs[0:64, 0:64])
        v.tensor_copy(lqi_t[0:64, 0:64], t_lqs2[0:64, 0:64])
        v.tensor_copy(lqr_t[64:128, 0:64], t_lqs[64:128, 64:128])
        return v.tensor_copy(lqi_t[64:128, 0:64], t_lqs2[64:128, 64:128])

    @p.dma("v_lamq", "sync", deps=["v_lamq0"])
    def _(eng, dma):
        dma(bass.AP(dlq, 0, [[64, 128], [1, 64]]), lqr)
    dma_counts["v_lamq"] = 1

    row_sl = [arena[0:1, 43008 + 512 * j:43008 + 512 * (j + 1)] for j in range(8)]
    c_ones = krow[0:1, 0:128]
    gv4 = lambda t: t.rearrange("p (i b n) -> p i b n", i=16, b=2)
    def pview(t, j):
        b, half = j // 2, j % 2
        return gv4(t)[:, 8 * half:8 * half + 8, b]

    def prow(T, qt, j):
        b, half = j // 2, j % 2
        return bass.AP(dpar, T * 8192 + (b * 64 + qt * 16 + 8 * half) * 64,
                       [[1, 512]])

    psrc3 = lambda pc: ps[2 + pc][:, 0:512].rearrange("p (a n) -> p a n", a=8)

    for qt in range(NQT):
        qx = str(qt)
        rbase = qt * 16 * 64

        @p.dma(f"rs_s{qx}", "sync",
               deps=["bounce"] + ([f"bccr{qt - 1}"] if qt else []))
        def _(eng, dma, qt=qt):
            for pc in range(4):
                dma(row_sl[pc], prow(0, qt, pc))
        dma_counts[f"rs_s{qx}"] = 4

        @p.step(f"bcs{qx}", "tensor",
                deps=[f"rs_s{qx}", "ld_par", "c_ones_i"] +
                ([f"genQ2{qt - 1}"] if qt else []))
        def _(te):
            for pc in range(4):
                last = te.matmul(ps[2 + pc][:, 0:512], c_ones,
                                 row_sl[pc][:, 0:512], start=True, stop=True)
            return last

        @p.step(f"genA{qx}", "scalar", deps=[f"bcs{qx}"] +
                ([f"genQ2{qt - 1}"] if qt else []))
        def _(eng):
            for pc in range(4):
                last = eng.activation(pview(g_arg, pc), psrc3(pc),
                                      AF.Copy, scale=t_ccol[:, 0:1])
            return last

        @p.dma(f"rs_t{qx}", "sync",
               deps=["bounce"] + ([f"bcci{qt - 1}"] if qt else []))
        def _(eng, dma, qt=qt):
            for pc in range(4):
                dma(row_sl[4 + pc], prow(1, qt, pc))
        dma_counts[f"rs_t{qx}"] = 4

        @p.step(f"bct{qx}", "tensor", deps=[f"rs_t{qx}", f"genA{qx}"])
        def _(te):
            for pc in range(4):
                last = te.matmul(ps[2 + pc][:, 0:512], c_ones,
                                 row_sl[4 + pc][:, 0:512], start=True, stop=True)
            return last

        @p.step(f"genBt{qx}", "scalar", deps=[f"bct{qx}"] +
                ([f"genT{qt - 1}"] if qt else []))
        def _(eng):
            for pc in range(4):
                last = eng.activation(pview(g_t1, pc), psrc3(pc),
                                      AF.Copy, scale=t_ccol[:, 0:1])
            return last

        @p.step(f"genB{qx}", "vector", deps=[f"genBt{qx}"])
        def _(v):
            TS, STT = v.tensor_scalar, v.scalar_tensor_tensor
            g_i2 = g_mag.bitcast(I32)
            TS(g_t2, g_t1, INV2PI, 0.0, ALU.mult, ALU.add)
            TS(g_t3, g_t1, INV2PI, 0.25, ALU.mult, ALU.add)
            v.tensor_copy(g_i, g_t2)
            v.tensor_copy(g_i2, g_t3)
            v.tensor_copy(g_t2, g_i)
            v.tensor_copy(g_t3, g_i2)
            STT(g_t2, g_t2, -TWOPI, g_t1, ALU.mult, ALU.add)
            return STT(g_t3, g_t3, -TWOPI, g_t1, ALU.mult, ALU.add)

        @p.step(f"genT{qx}", "scalar", deps=[f"genB{qx}", f"genA{qx}"])
        def _(eng):
            eng.activation(g_mag, g_arg, AF.Exp)
            eng.activation(g_t3, g_t3, AF.Sin, bias=t_ccol[:, 1:2])
            return eng.activation(g_t2, g_t2, AF.Sin)

        @p.step(f"genP{qx}", "vector", deps=[f"genT{qx}"] +
                ([] if qt == 0 else [f"tE{qt - 1}_7"]))
        def _(v):
            v.tensor_tensor(g_tre, g_mag, g_t3, ALU.mult)
            return v.tensor_tensor(g_tim, g_mag, g_t2, ALU.mult)

        @p.dma(f"rs_cr{qx}", "sync", deps=[f"bcs{qx}"])
        def _(eng, dma, qt=qt):
            for pc in range(4):
                dma(row_sl[pc], prow(2, qt, pc))
        dma_counts[f"rs_cr{qx}"] = 4

        @p.step(f"bccr{qx}", "tensor", deps=[f"rs_cr{qx}", f"genBt{qx}"])
        def _(te):
            for pc in range(4):
                last = te.matmul(ps[2 + pc][:, 0:512], c_ones,
                                 row_sl[pc][:, 0:512], start=True, stop=True)
            return last

        @p.step(f"genQ1{qx}", "vector", deps=[f"bccr{qx}", f"genP{qx}"])
        def _(v):
            for pc in range(4):
                v.tensor_tensor(pview(g_arg, pc), psrc3(pc), pview(g_tre, pc),
                                ALU.mult)
            for pc in range(4):
                last = v.tensor_tensor(pview(g_rep, pc), psrc3(pc),
                                       pview(g_tim, pc), ALU.mult)
            return last

        @p.dma(f"rs_ci{qx}", "sync", deps=[f"bct{qx}"])
        def _(eng, dma, qt=qt):
            for pc in range(4):
                dma(row_sl[4 + pc], prow(3, qt, pc))
        dma_counts[f"rs_ci{qx}"] = 4

        @p.step(f"bcci{qx}", "tensor", deps=[f"rs_ci{qx}", f"genQ1{qx}"])
        def _(te):
            for pc in range(4):
                last = te.matmul(ps[2 + pc][:, 0:512], c_ones,
                                 row_sl[4 + pc][:, 0:512], start=True, stop=True)
            return last

        @p.step(f"genQ2{qx}", "vector", deps=[f"bcci{qx}"])
        def _(v, qt=qt):
            TT = v.tensor_tensor
            for pc in range(4):
                TT(pview(g_mag, pc), psrc3(pc), pview(g_tim, pc), ALU.mult)
            for pc in range(4):
                TT(pview(g_t3, pc), psrc3(pc), pview(g_tre, pc), ALU.mult)
            sl = lambda t, b: t.rearrange("p (i b n) -> p b i n", i=16, b=2)[:, b]
            for b in range(2):
                co = b * 4096 + qt * 1024
                TT(p2re[:, co:co + 1024].rearrange("p (i n) -> p i n", i=16),
                   sl(g_arg, b), sl(g_mag, b), ALU.subtract)
                last = TT(p2im[:, co:co + 1024].rearrange("p (i n) -> p i n", i=16),
                          sl(g_rep, b), sl(g_t3, b), ALU.add)
            return last

        for gidx in range(8):
            war = []
            if not (qt == 0 and gidx < 2):
                war = ([f"eE{qt}_{gidx - 2}"] if gidx >= 2 else [f"eE{qt - 1}_{gidx + 6}"])
            psx = [ps_a, ps_b][gidx % 2]

            @p.step(f"tE{qx}_{gidx}", "tensor",
                    deps=[f"genP{qx}", "v_lamq"] + war)
            def _(te, gidx=gidx, psx=psx):
                for k in range(2):
                    i = gidx * 2 + k
                    te.transpose(psx[:, k * 256:k * 256 + 128],
                                 g_tre[:, i * 128:(i + 1) * 128], t_id[:])
                    te.transpose(psx[:, k * 256 + 128:k * 256 + 256],
                                 g_tim[:, i * 128:(i + 1) * 128], t_id[:])
                return te.matmul(ps_y[0][:, 0:128], t_anti[:], xt[:, 0:128],
                                 start=True, stop=True)

            @p.step(f"eE{qx}_{gidx}", "scalar", deps=[f"tE{qx}_{gidx}"])
            def _(eng, qt=qt, gidx=gidx, psx=psx):
                for k in range(2):
                    i = gidx * 2 + k
                    col = (qt * 16 + i) * 128
                    eng.activation(ere[:, col + 1:col + 128],
                                   psx[:, k * 256:k * 256 + 127], AF.Copy)
                    last = eng.activation(eim[:, col + 1:col + 128],
                                          psx[:, k * 256 + 128:k * 256 + 255],
                                          AF.Copy, scale=-1.0)
                return last

    @p.step("einit", "vector", deps=["eE3_7"])
    def _(v):
        ere_r = ere.rearrange("p (a b) -> p a b", a=64)
        eim_r = eim.rearrange("p (a b) -> p a b", a=64)
        v.memset(ere_r[:, :, 0:1], 1.0)
        return v.memset(eim_r[:, :, 0:1], 0.0)

    @p.step("genK", "vector", deps=["genQ23", "prep2", "bcci3"])
    def _(v):
        v.reduce_sum(ksub[:], p2re.rearrange("p (h n) -> p h n", h=H), AX.X)
        v.memset(krow[:, 0:127], 0.0)
        return v.tensor_copy(krow[:, 127:128], t_k0[:])

    @p.step("t_ksub", "tensor", deps=["genK", "eE3_7"])
    def _(te):
        te.transpose(ps_a[:, 0:128], ksub[:], t_id[:])
        return te.matmul(ps_a[:, 256:384], t_anti[:], xt[:, 0:128],
                         start=True, stop=True)

    @p.step("e_ksub", "scalar", deps=["t_ksub"])
    def _(eng):
        return eng.activation(krow[:, 128:256], ps_a[:, 0:128], AF.Copy)

    @p.dma("w_kpad", "sync", deps=["e_ksub", "genK"])
    def _(eng, dma):
        dma(dkpad.ap(), krow[:])
    dma_counts["w_kpad"] = 1

    @p.dma("toepd", "gpsimd", deps=["w_kpad", "tE3_7", "genQ23"])
    def _(eng, dma):
        for h in range(H):
            src = bass.AP(dkpad, h * 256, [[1, 128], [1, 128]])
            dma(toep[:, h * 128:(h + 1) * 128], src)
    dma_counts["toepd"] = 128

    for i in range(32):
        war = ["genQ23", "e_ksub"] if i < 2 else [f"exf{i - 2}"]
        psx = [ps_a, ps_b][i % 2]

        @p.step(f"txf{i}", "tensor", deps=["ld_x", "ld_par"] + war)
        def _(te, i=i, psx=psx):
            return te.matmul(psx[:], t_anti[:], xt[:, i * 512:(i + 1) * 512],
                             start=True, stop=True)

        @p.step(f"exf{i}", "scalar", deps=[f"txf{i}"])
        def _(eng, i=i, psx=psx):
            return eng.activation(xf[:, i * 512:(i + 1) * 512], psx[:], AF.Copy)

    for hq in range(32):
        h0 = hq * 4
        grp = 0 if h0 < 64 else 1
        war = [] if hq < 2 else [f"ue{hq - 2}"]
        if hq >= 18 and grp == 1:
            war += [f"um{hq - 2}"]
        pu_re = ps_u[(hq % 2) * 2]
        pu_im = ps_u[(hq % 2) * 2 + 1]

        @p.step(f"u{hq}", "tensor", deps=["exf31"] + war)
        def _(te, h0=h0, pu_re=pu_re, pu_im=pu_im):
            for k in range(4):
                h = h0 + k
                p2r_s = p2re[:, h * 64:(h + 1) * 64]
                p2i_s = p2im[:, h * 64:(h + 1) * 64]
                xf_s = xf[:, h::128]
                te.matmul(pu_re[0:64, k * 128:(k + 1) * 128],
                          p2r_s, xf_s, start=True, stop=True)
                last = te.matmul(pu_im[0:64, k * 128:(k + 1) * 128],
                                 p2i_s, xf_s, start=True, stop=True)
            return last

        if grp == 0:
            @p.step(f"ue{hq}", "vector", deps=[f"u{hq}"])
            def _(v, h0=h0, pu_re=pu_re, pu_im=pu_im):
                slot = (h0 % 64) * 128
                v.tensor_copy(ure[0:64, slot:slot + 512], pu_re[0:64, :])
                return v.tensor_copy(uim[0:64, slot:slot + 512], pu_im[0:64, :])
        else:
            par = hq % 2

            @p.step(f"ue{hq}", "vector", deps=[f"u{hq}"])
            def _(v, pu_re=pu_re, pu_im=pu_im, par=par):
                v.tensor_copy(u_stage[0][0:64, par * 512:(par + 1) * 512],
                              pu_re[0:64, :])
                return v.tensor_copy(u_stage[1][0:64, par * 512:(par + 1) * 512],
                                     pu_im[0:64, :])

            @p.dma(f"um{hq}", "gpsimd", deps=[f"ue{hq}"])
            def _(eng, dma, h0=h0, par=par):
                slot = (h0 % 64) * 128
                dma(ure[64:128, slot:slot + 512],
                    u_stage[0][0:64, par * 512:(par + 1) * 512])
                dma(uim[64:128, slot:slot + 512],
                    u_stage[1][0:64, par * 512:(par + 1) * 512])
            dma_counts[f"um{hq}"] = 2

    @p.step("scan_in", "vector", deps=["ue31", "u31", "um31", "v_lamq"])
    def _(v):
        v.memset(wbr_r[:, :, 0::64], 0.0)
        v.memset(wbi_r[:, :, 0::64], 0.0)
        v.memset(wst[0][:], 0.0)
        return v.memset(wst[1][:], 0.0)

    for jb in range(1, 64):
        g = jb - 1
        gdeps = ["scan_in"] if g < 3 else [f"scpy{g - 3}"]

        @p.step(f"scan{g}", "vector", deps=gdeps)
        def _(v, jb=jb, g=g):
            TT = v.tensor_tensor
            if g == 0:
                wp_r3, wp_i3 = wst3[0], wst3[1]
            else:
                ps_ = (g - 1) % 3
                wp_r3, wp_i3 = wst3_r[ps_], wst3_i[ps_]
            ns_ = g % 3
            wn_r3, wn_i3 = wst3_r[ns_], wst3_i[ns_]
            up_r = ure_r[:, :, jb - 1::64]
            up_i = uim_r[:, :, jb - 1::64]
            TT(s3[0], wp_r3, lqr_b, ALU.mult)
            TT(s3[1], wp_i3, lqi_b, ALU.mult)
            TT(s3[2], wp_r3, lqi_b, ALU.mult)
            TT(s3[3], wp_i3, lqr_b, ALU.mult)
            TT(s3[0], s3[0], s3[1], ALU.subtract)
            TT(s3[2], s3[2], s3[3], ALU.add)
            TT(wn_r3, s3[0], up_r, ALU.add)
            return TT(wn_i3, s3[2], up_i, ALU.add)

        @p.step(f"scpy{g}", "scalar", deps=[f"scan{g}"])
        def _(eng, jb=jb, g=g):
            ns_ = g % 3
            eng.activation(wbr_r[:, :, jb::64], wst3_r[ns_], AF.Copy)
            return eng.activation(wbi_r[:, :, jb::64], wst3_i[ns_], AF.Copy)

    @p.step("scan", "vector", deps=["scpy62"])
    def _(v):
        return v.tensor_copy(t_dmy[:], t_dt[:].broadcast_to([H, N]))

    for hq in range(32):
        h0 = hq * 4
        pbase = 0 if h0 < 64 else 64
        war = [] if hq < 2 else [f"ye{hq - 2}"]
        psy = ps_y[hq % 2]

        @p.step(f"y{hq}", "tensor", deps=["scan", "toepd", "einit"] + war)
        def _(te, h0=h0, pbase=pbase, psy=psy):
            for k in range(4):
                h = h0 + k
                hh = h % 64
                xf_s = xf[:, h::128]
                tp_s = toep[:, h * 128:(h + 1) * 128]
                wr_s = wbr[pbase:pbase + 64, hh * 128:(hh + 1) * 128]
                wi_s = wbi[pbase:pbase + 64, hh * 128:(hh + 1) * 128]
                er_s = ere[pbase:pbase + 64, hh * 128:(hh + 1) * 128]
                ei_s = eim[pbase:pbase + 64, hh * 128:(hh + 1) * 128]
                out = psy[:, k * 128:(k + 1) * 128]
                te.matmul(out, xf_s, tp_s, start=True, stop=False)
                te.matmul(out, wr_s, er_s, start=False, stop=False)
                last = te.matmul(out, wi_s, ei_s, start=False, stop=True)
            return last

        gi = hq // 2
        wary = [] if gi < 2 else [f"ysend{gi - 2}", f"yhalo{gi - 2}"]

        @p.step(f"ye{hq}", "scalar", deps=[f"y{hq}"] + wary)
        def _(eng, hq=hq, psy=psy):
            buf = ybuf[(hq // 2) % 2]
            off = (hq % 2) * 512
            return eng.activation(buf[:, off:off + 512], psy[:], AF.Copy)

    for gi in range(16):
        ysq = ["sync", "gpsimd"][gi % 2]

        @p.dma(f"ysend{gi}", ysq, deps=[f"ye{2 * gi}", f"ye{2 * gi + 1}"])
        def _(eng, dma, gi=gi):
            h0 = gi * 8
            buf = ybuf[gi % 2]
            bufv = buf.rearrange("p (c e) -> p c e", c=8)
            for s in range(8):
                dst = bass.AP(dyb, s * 128 * 2049 + h0 * 2049 + 1,
                              [[128, 16], [2049, 8], [1, 128]])
                dma(dst, bufv[16 * s:16 * (s + 1)])

        @p.dma(f"yhalo{gi}", "sync", deps=[f"ye{2 * gi}", f"ye{2 * gi + 1}"])
        def _(eng, dma, gi=gi):
            h0 = gi * 8
            buf = ybuf[gi % 2]
            for s in range(1, 8):
                dst = bass.AP(dyb, s * 128 * 2049 + h0 * 2049,
                              [[1, 1], [2049, 8]])
                dma(dst, buf[16 * s - 1:16 * s, 127::128],
                    allow_slow_non_contiguous=True)
        dma_counts[f"ysend{gi}"] = 8
        dma_counts[f"yhalo{gi}"] = 7

    @p.dma("yzero", "sync", deps=["prep1"])
    def _(eng, dma):
        dma(bass.AP(dyb, 0, [[2049, 128], [1, 1]]), zcol[:, 0:1],
            allow_slow_non_contiguous=True)
    dma_counts["yzero"] = 1

    @p.step("a2a", "gpsimd",
            deps=["yzero"] + [f"ysend{gi}" for gi in range(16)]
            + [f"yhalo{gi}" for gi in range(16)])
    def _(g):
        return g.collective_compute(
            "AllToAll", ALU.bypass,
            replica_groups=[[0, 1, 2, 3, 4, 5, 6, 7]],
            ins=[io["dyb"].ap().opt()], outs=[io["ga"].ap().opt()])


def build_main(nc, ctx, p, io, dc, arena, ps):
    sb = lambda name, shape, dt_: ctx.enter_context(nc.sbuf_tensor(name, shape, dt_))

    def f32v(off, w):
        return arena[:, MAIN0 + off:MAIN0 + off + w]

    def bf16v(off, w_f32):
        return arena[:, MAIN0 + off:MAIN0 + off + w_f32].bitcast(BF16)

    g3 = lambda t: t.rearrange("p (g n) -> p g n", g=NG)
    xs_sl = [g3(f32v(0, 4096)), g3(f32v(4096, 4096))]
    ub = g3(f32v(8192, 4096))
    y1l = g3(f32v(12288, 4096))
    x1c = g3(f32v(22528, 4096))
    sgl = g3(f32v(16384, 4096))
    gb = g3(bf16v(20480, 2048))
    ycb = gb
    cb1 = g3(bf16v(22528, 2048))
    cb2 = g3(bf16v(24576, 2048))
    x1f = f32v(0, 4104).rearrange("p (g n) -> p g n", g=NG)
    xrb = g3(bf16v(4112, 2048))
    xkb = g3(bf16v(6160, 2048))
    kactb = g3(bf16v(8208, 2048))
    r2sq = g3(bf16v(8208, 2048))
    r_p = [f32v(10256, 512), f32v(10768, 512)]
    mixt = g3(f32v(11280, 4096))
    mixf = g3(f32v(15376, 4096))
    scr1 = mixf
    r2 = g3(f32v(19472, 4096))
    r2b = g3(bf16v(23568, 2048))
    x1r = [f32v(25616, 512), f32v(26128, 512)]
    wout_sb = arena[:, 0:8192].bitcast(BF16)
    wr_sb = arena[:, 8192:12288].bitcast(BF16)
    wk_sb = arena[:, 12288:16384].bitcast(BF16)
    wv_sb = arena[:, 16384:20480].bitcast(BF16)

    dvec = sb("dvec", [128, 8], F32)
    bvec = sb("bvec", [128, 16], F32)
    gvec = sb("gvec", [128, 8], F32)
    betv = sb("betv", [128, 8], F32)
    murv = sb("murv", [128, 8], F32)
    mukv = sb("mukv", [128, 8], F32)
    murm = sb("murm", [128, 8], F32)
    mukm = sb("mukm", [128, 8], F32)
    tcon = sb("tcon", [128, 8], F32)
    onesb = sb("onesb", [128, 128], BF16)
    onesf = sb("onesf", [1, 128], F32)
    bm = sb("bm", [128, 512], F32)
    bs = sb("bs", [128, 512], F32)
    t_spc = sb("t_spc", [1, 8], F32)

    nc.const_aps.register(F32, 1e-5, tcon[:, 2:3])

    xs_d = io["xs"]; ga_d = io["ga"]
    dx1 = io["dx1"]; dstats = io["dstats"]; xout = io["xout"]

    def SP(v):
        v.tensor_scalar(t_spc[:], t_spc[:], 1.0, None, ALU.mult)

    @p.dma("m_wload", "gpsimd", deps=["y31"])
    def _(eng, dma):
        for gq in range(NG):
            dma(wout_sb[:, gq * 2048:(gq + 1) * 2048],
                bass.AP(io["wout"], gq * 128 * 2048, [[2048, 128], [1, 2048]]))
    dc["m_wload"] = 8

    @p.dma("m_wload2", "gpsimd", deps=["y31"])
    def _(eng, dma):
        for wsb, wd in [(wr_sb, io["wr"]), (wk_sb, io["wk"]), (wv_sb, io["wv"])]:
            for gq in range(NG):
                dma(wsb[:, gq * 1024:(gq + 1) * 1024],
                    bass.AP(wd, gq * 128 * 1024, [[1024, 128], [1, 1024]]))
    dc["m_wload2"] = 24

    @p.dma("m_vload", "sync", deps=["a2a"])
    def _(eng, dma):
        slow = dict(allow_slow_non_contiguous=True)
        dma(dvec[:], bass.AP(io["dvecd"], 0, [[1, 128], [128, 8]]), **slow)
        dma(bvec[:], bass.AP(io["bvecd"], 0, [[1, 128], [128, 16]]), **slow)
        dma(gvec[:], bass.AP(io["gvecd"], 0, [[1, 128], [128, 8]]), **slow)
        dma(betv[:], bass.AP(io["betvd"], 0, [[1, 128], [128, 8]]), **slow)
        dma(murv[:], bass.AP(io["murd"], 0, [[1, 128], [128, 8]]), **slow)
        dma(mukv[:], bass.AP(io["mukd"], 0, [[1, 128], [128, 8]]), **slow)
        dma(tcon[:], io["consts"].ap())
        dma(onesb[:], io["onesd"].ap())
        dma(onesf[:], io["onesfd"].ap())
    dc["m_vload"] = 9

    @p.step("m_veci", "vector", deps=["m_vload"])
    def _(v):
        v.tensor_scalar(murm[:], murv[:], -1.0, 1.0, ALU.mult, ALU.add)
        v.memset(t_spc[:], 1.0)
        return v.tensor_scalar(mukm[:], mukv[:], -1.0, 1.0, ALU.mult, ALU.add)

    def stats(tag, srcb, srcsq, depc, bcdep, n, smdep=(), depc2=None):
        @p.step(f"sm{tag}a", "tensor", deps=[depc] + list(smdep) + bcdep)
        def _(te, n=n, srcb=srcb):
            for gq in range(NG):
                last = te.matmul(ps[4][:, 0:n], onesb[:], srcb[:, gq, 0:n],
                                 start=(gq == 0), stop=(gq == 7))
            return last

        @p.step(f"sm{tag}b", "tensor", deps=[depc2 or depc, f"sm{tag}a"])
        def _(te, n=n, srcsq=srcsq):
            for gq in range(NG):
                last = te.matmul(ps[5][:, 0:n], onesb[:], srcsq[:, gq, 0:n],
                                 start=(gq == 0), stop=(gq == 7))
            return last

        @p.step(f"pp{tag}v", "vector", deps=[f"sm{tag}a", f"sm{tag}b"])
        def _(v, n=n):
            zb = tcon[:, 3:4].broadcast_to([128, n])
            v.tensor_tensor(bm[:, 0:n], ps[4][:, 0:n], zb, ALU.add)
            v.tensor_tensor(bs[:, 0:n], ps[5][:, 0:n], zb, ALU.add)
            v.tensor_tensor(ps[4][:, 0:n], bm[:, 0:n], bm[:, 0:n], ALU.mult)
            SP(v)
            return v.tensor_tensor(bs[:, 0:n], bs[:, 0:n], ps[4][:, 0:n],
                                   ALU.subtract)

        @p.step(f"pp{tag}s", "scalar", deps=[f"pp{tag}v"])
        def _(eng, n=n):
            return eng.activation(bs[:, 0:n], bs[:, 0:n], AF.Sqrt,
                                  bias=tcon[:, 2:3])
        return f"pp{tag}s"

    def a_names(ci):
        cn = CHUNKS[ci][0]
        pv = CHUNKS[ci - 1][0] if ci >= 1 else None
        pv2 = CHUNKS[ci - 2][0] if ci >= 2 else None
        return cn, pv, pv2

    def attn_front(ci):
        cn, pv, pv2 = a_names(ci)
        C = cn
        off, n = CHUNKS[ci][1], CHUNKS[ci][2]
        xs_c = xs_sl[ci % 2]
        dxa_dep = ["m_veci", "y31"] + ([f"r1{pv2}"] if pv2 else [])
        dya_dep = ["a2a"] + ([f"mm{pv}_15"] if pv else [])

        @p.dma(f"dxa{C}", "sync", deps=dxa_dep)
        def _(eng, dma, off=off, n=n, xs_c=xs_c):
            dma(xs_c[:, :, 0:n],
                bass.AP(xs_d, off, [[TH, 128], [128 * TH, 8], [1, n]]),
                **(dict(allow_slow_non_contiguous=True) if n == 1 else {}))
        dc[f"dxa{C}"] = 1

        @p.dma(f"dya{C}", "gpsimd", deps=dya_dep)
        def _(eng, dma, off=off, n=n):
            dma(ycb[:, :, 0:n],
                bass.AP(ga_d, off, [[2049, 128], [128 * 2049, 8], [1, n]]),
                **(dict(allow_slow_non_contiguous=True) if n == 1 else {}))
        dc[f"dya{C}"] = 1

        @p.step(f"ud{C}", "scalar", deps=[f"dxa{C}"] + ([f"gel{pv}"] if pv else []))
        def _(eng, n=n, xs_c=xs_c):
            for gq in range(NG):
                last = eng.activation(ub[:, gq, 0:n], xs_c[:, gq, 0:n], AF.Copy,
                                      scale=dvec[:, gq:gq + 1])
            return last

        @p.step(f"uu{C}", "vector", deps=[f"ud{C}", f"dya{C}"])
        def _(v, n=n):
            for h in (slice(0, 4), slice(4, 8)):
                last = v.tensor_tensor(ub[:, h, 0:n], ub[:, h, 0:n],
                                       ycb[:, h, 0:n], ALU.add)
            return last

        gel_dep = [f"uu{C}"] + ([f"mm{pv}_15"] if pv else ["m_wload"])

        @p.step(f"gel{C}", "scalar", deps=gel_dep)
        def _(eng, n=n):
            return eng.activation(gb[:, :, 0:n], ub[:, :, 0:n], AF.Gelu_apprx_tanh)

    def attn_body(ci):
        cn, pv, pv2 = a_names(ci)
        C = cn
        off, n = CHUNKS[ci][1], CHUNKS[ci][2]
        xs_c = xs_sl[ci % 2]
        prev = pv

        for m in range(16):
            bank = m % 4
            deps = [f"gel{C}"] + ([f"ev{C}_{m - 4}"] if m >= 4
                                  else ([f"ev{prev}_{m + 12}"] if prev else []))

            @p.step(f"mm{C}_{m}", "tensor", deps=deps)
            def _(te, m=m, bank=bank, n=n):
                for gq in range(NG):
                    last = te.matmul(
                        ps[bank][:, 0:n],
                        wout_sb[:, gq * 2048 + m * 128:gq * 2048 + (m + 1) * 128],
                        gb[:, gq, 0:n], start=(gq == 0), stop=(gq == 7))
                return last

            evdep = [f"mm{C}_{m}"]
            if prev:
                evdep += [f"gam{prev}"] if m < 8 else [f"x1{prev}"]
            if m < 8:
                @p.step(f"ev{C}_{m}", "scalar", deps=evdep)
                def _(eng, m=m, bank=bank, n=n):
                    return eng.activation(y1l[:, m, 0:n], ps[bank][:, 0:n],
                                          AF.Identity, bias=bvec[:, m:m + 1])
            else:
                @p.step(f"ev{C}_{m}", "scalar", deps=evdep)
                def _(eng, m=m, bank=bank, n=n):
                    return eng.activation(sgl[:, m - 8, 0:n], ps[bank][:, 0:n],
                                          AF.Sigmoid, bias=bvec[:, m:m + 1])

        @p.step(f"glu{C}", "vector", deps=[f"ev{C}_{14}", f"ev{C}_{15}",
                                           f"ev{C}_6", f"ev{C}_7"])
        def _(v, n=n):
            for h in (slice(0, 4), slice(4, 8)):
                last = v.tensor_tensor(y1l[:, h, 0:n], y1l[:, h, 0:n],
                                       sgl[:, h, 0:n], ALU.mult)
            return last

        cst_dep = [f"glu{C}"] + ([f"smo{prev}b", f"dxo{prev}", f"dxoa{prev}"] if prev else [])

        @p.step(f"cst{C}", "scalar", deps=cst_dep)
        def _(eng, n=n):
            return eng.activation(cb2[:, :, 0:n], y1l[:, :, 0:n], AF.Square)

        @p.step(f"cstv{C}", "vector", deps=cst_dep)
        def _(v, n=n):
            return v.tensor_scalar(cb1[:, :, 0:n], y1l[:, :, 0:n], 1.0, None, ALU.mult)

        sdone = stats(f"i{C}", cb1, cb2, f"cstv{C}", ([f"x1{pv}"] if pv else []), n,
                      smdep=([f"ppo{pv}v"] if pv else []), depc2=f"cst{C}")

        if ci + 1 < len(CHUNKS):
            attn_front(ci + 1)

        @p.step(f"api{C}", "vector", deps=[sdone])
        def _(v, n=n):
            bmb = lambda: bm[:, 0:n].unsqueeze(1).broadcast_to([128, 4, n])
            bsb = lambda: bs[:, 0:n].unsqueeze(1).broadcast_to([128, 4, n])
            v.reciprocal(bs[:, 0:n], bs[:, 0:n])
            for h in (slice(0, 4), slice(4, 8)):
                v.tensor_tensor(y1l[:, h, 0:n], y1l[:, h, 0:n], bmb(), ALU.subtract)
            for h in (slice(0, 4), slice(4, 8)):
                last = v.tensor_tensor(y1l[:, h, 0:n], y1l[:, h, 0:n], bsb(),
                                       ALU.mult)
            return last

        @p.step(f"gam{C}", "scalar", deps=[f"api{C}"])
        def _(eng, n=n):
            for gq in range(NG):
                last = eng.activation(sgl[:, gq, 0:n], y1l[:, gq, 0:n],
                                      AF.Identity, scale=gvec[:, gq:gq + 1],
                                      bias=betv[:, gq:gq + 1])
            return last

        @p.step(f"r1{C}", "vector", deps=[f"gam{C}"])
        def _(v, n=n, xs_c=xs_c):
            for h in (slice(0, 4), slice(4, 8)):
                last = v.tensor_tensor(sgl[:, h, 0:n], sgl[:, h, 0:n],
                                       xs_c[:, h, 0:n], ALU.add)
            return last

        @p.step(f"c1{C}", "scalar", deps=[f"r1{C}"])
        def _(eng, n=n):
            return eng.activation(cb2[:, :, 0:n], sgl[:, :, 0:n], AF.Square)

        @p.step(f"c1v{C}", "vector", deps=[f"r1{C}"])
        def _(v, n=n):
            return v.tensor_scalar(cb1[:, :, 0:n], sgl[:, :, 0:n], 1.0, None, ALU.mult)

        sdone1 = stats(f"o{C}", cb1, cb2, f"c1v{C}", [f"api{C}"], n,
                       smdep=[f"ppi{C}v"], depc2=f"c1{C}")

        @p.step(f"x1a{C}", "vector", deps=[sdone1])
        def _(v, n=n, is_halo=(ci == 0)):
            bmb = bm[:, 0:n].unsqueeze(1).broadcast_to([128, 4, n])
            bsb = bs[:, 0:n].unsqueeze(1).broadcast_to([128, 4, n])
            v.reciprocal(bs[:, 0:n], bs[:, 0:n])
            v.tensor_tensor(x1c[:, 0:4, 0:n], sgl[:, 0:4, 0:n], bmb, ALU.subtract)
            SP(v)
            last = v.tensor_tensor(x1c[:, 0:4, 0:n], x1c[:, 0:4, 0:n], bsb,
                                   ALU.mult)
            if is_halo:
                mb = tcon[:, 0:1].unsqueeze(1).broadcast_to([128, 4, n])
                SP(v)
                last = v.tensor_tensor(x1c[:, 0:4, 0:n], x1c[:, 0:4, 0:n], mb,
                                       ALU.mult)
            return last

        @p.dma(f"dxoa{C}", "sync", deps=[f"x1a{C}"])
        def _(eng, dma, off=off, n=n):
            dma(bass.AP(dx1, off, [[2049, 128], [128 * 2049, 4], [1, n]]),
                x1c[:, 0:4, 0:n],
                **(dict(allow_slow_non_contiguous=True) if n == 1 else {}))
        dc[f"dxoa{C}"] = 1

        @p.step(f"x1{C}", "vector", deps=[f"x1a{C}"])
        def _(v, n=n, is_halo=(ci == 0)):
            bmb = bm[:, 0:n].unsqueeze(1).broadcast_to([128, 4, n])
            bsb = bs[:, 0:n].unsqueeze(1).broadcast_to([128, 4, n])
            v.tensor_tensor(x1c[:, 4:8, 0:n], sgl[:, 4:8, 0:n], bmb, ALU.subtract)
            SP(v)
            last = v.tensor_tensor(x1c[:, 4:8, 0:n], x1c[:, 4:8, 0:n], bsb,
                                   ALU.mult)
            if is_halo:
                mb = tcon[:, 0:1].unsqueeze(1).broadcast_to([128, 4, n])
                SP(v)
                last = v.tensor_tensor(x1c[:, 4:8, 0:n], x1c[:, 4:8, 0:n], mb,
                                       ALU.mult)
            return last

        @p.dma(f"dxo{C}", "sync", deps=[f"x1{C}"])
        def _(eng, dma, off=off, n=n):
            dma(bass.AP(dx1, 4 * 128 * 2049 + off, [[2049, 128], [128 * 2049, 4], [1, n]]),
                x1c[:, 4:8, 0:n],
                **(dict(allow_slow_non_contiguous=True) if n == 1 else {}))
        dc[f"dxo{C}"] = 1

    attn_front(0)
    for ci in range(len(CHUNKS)):
        attn_body(ci)

    def ffn_front(ci):
        cn, off, n = FCHUNKS[ci]
        C = "f" + cn
        pf = f"f{FCHUNKS[ci - 1][0]}" if ci else None
        dxf_extra = {0: ["dxoh", "dxoah", "dxo0", "dxoa0", "r13", "r12"],
                     1: ["dxo1", "dxoa1"], 2: ["dxo2", "dxoa2"],
                     3: ["dxo3", "dxoa3"]}[ci]

        @p.dma(f"dxf{C}", "sync", deps=dxf_extra + ([f"vxk{pf}"] if pf else []))
        def _(eng, dma, off=off, n=n):
            dma(x1f[:, :, 0:n + 1],
                bass.AP(dx1, off, [[2049, 128], [128 * 2049, 8], [1, n + 1]]))
        dc[f"dxf{C}"] = 1

        @p.step(f"mxr{C}", "scalar", deps=[f"dxf{C}"] +
                ([f"kact{pf}"] if pf else ["x13", "dxo3", "dxoa3"]))
        def _(eng, n=n):
            for gq in range(NG):
                eng.activation(mixf[:, gq, 0:n], x1f[:, gq, 1:n + 1], AF.Copy,
                               scale=murv[:, gq:gq + 1])
            for gq in range(NG):
                last = eng.activation(mixt[:, gq, 0:n], x1f[:, gq, 0:n], AF.Copy,
                                      scale=murm[:, gq:gq + 1])
            return last

        @p.step(f"vxr{C}", "vector", deps=[f"mxr{C}"] +
                ([f"rm{pf}_7"] if pf else []))
        def _(v, n=n):
            for h in (slice(0, 4), slice(4, 8)):
                last = v.tensor_tensor(xrb[:, h, 0:n], mixf[:, h, 0:n],
                                       mixt[:, h, 0:n], ALU.add)
            return last

        @p.step(f"mxk{C}", "scalar", deps=[f"vxr{C}"])
        def _(eng, n=n):
            for gq in range(NG):
                eng.activation(mixf[:, gq, 0:n], x1f[:, gq, 1:n + 1], AF.Copy,
                               scale=mukv[:, gq:gq + 1])
            for gq in range(NG):
                last = eng.activation(mixt[:, gq, 0:n], x1f[:, gq, 0:n], AF.Copy,
                                      scale=mukm[:, gq:gq + 1])
            return last

        @p.step(f"vxk{C}", "vector", deps=[f"mxk{C}"] +
                ([f"km{pf}_7"] if pf else []))
        def _(v, n=n):
            for h in (slice(0, 4), slice(4, 8)):
                last = v.tensor_tensor(xkb[:, h, 0:n], mixf[:, h, 0:n],
                                       mixt[:, h, 0:n], ALU.add)
            return last

    def ffn_body(ci):
        cn, off, n = FCHUNKS[ci]
        C = "f" + cn
        pf = f"f{FCHUNKS[ci - 1][0]}" if ci else None

        for m in range(NG):
            bank = m % 2
            deps = [f"vxk{C}", "m_wload2"] + ([f"kev{C}_{m - 2}"] if m >= 2
                    else ([f"kev{pf}_{m + 6}"] if pf else []))

            @p.step(f"km{C}_{m}", "tensor", deps=deps)
            def _(te, m=m, bank=bank, n=n):
                for gq in range(NG):
                    last = te.matmul(
                        ps[bank][:, 0:n],
                        wk_sb[:, gq * 1024 + m * 128:gq * 1024 + (m + 1) * 128],
                        xkb[:, gq, 0:n], start=(gq == 0), stop=(gq == 7))
                return last

            @p.step(f"kev{C}_{m}", "scalar", deps=[f"km{C}_{m}", f"vxk{C}"])
            def _(eng, m=m, bank=bank, n=n):
                return eng.activation(scr1[:, m, 0:n], ps[bank][:, 0:n], AF.Relu)

        @p.step(f"kact{C}", "scalar", deps=[f"kev{C}_7"] +
                ([f"smt{pf}b"] if pf else []))
        def _(eng, n=n):
            return eng.activation(kactb[:, :, 0:n], scr1[:, :, 0:n], AF.Square)

        for m in range(NG):
            @p.step(f"rm{C}_{m}", "tensor",
                    deps=[f"vxr{C}", "m_wload2"] + ([f"rev{C}_{m - 1}"] if m
                         else ([f"rev{pf}_7"] if pf else [])))
            def _(te, m=m, n=n):
                for gq in range(NG):
                    last = te.matmul(
                        ps[2][:, 0:n],
                        wr_sb[:, gq * 1024 + m * 128:gq * 1024 + (m + 1) * 128],
                        xrb[:, gq, 0:n], start=(gq == 0), stop=(gq == 7))
                return last

            @p.step(f"rev{C}_{m}", "scalar", deps=[f"rm{C}_{m}"] +
                    ([f"gt{C}_{m - 2}"] if m >= 2 else
                     ([f"gt{pf}_{m + 6}"] if pf else [])))
            def _(eng, m=m, n=n):
                return eng.activation(r_p[m % 2][:, 0:n], ps[2][:, 0:n], AF.Sigmoid)

            @p.step(f"vm{C}_{m}", "tensor",
                    deps=[f"kact{C}"] + ([f"gt{C}_{m - 1}"] if m
                         else ([f"gt{pf}_7"] if pf else [])))
            def _(te, m=m, n=n):
                for gq in range(NG):
                    last = te.matmul(
                        ps[3][:, 0:n],
                        wv_sb[:, gq * 1024 + m * 128:gq * 1024 + (m + 1) * 128],
                        kactb[:, gq, 0:n], start=(gq == 0), stop=(gq == 7))
                return last

            @p.dma(f"xld{C}_{m}", "gpsimd",
                   deps=[f"dxo{cn}", f"dxoa{cn}"]
                   + ([f"resm{C}_{m - 2}"] if m >= 2 else
                      ([f"resm{pf}_{m + 6}"] if pf else ["dxo3", "dxoa3"])))
            def _(eng, dma, m=m, off=off, n=n):
                dma(x1r[m % 2][:, 0:n],
                    bass.AP(dx1, off + 1 + m * 128 * 2049,
                            [[2049, 128], [1, n]]))
            dc[f"xld{C}_{m}"] = 1

            @p.step(f"gt{C}_{m}", "vector", deps=[f"vm{C}_{m}", f"rev{C}_{m}"] +
                    ([f"dout{pf}"] if pf and m == 0 else []))
            def _(v, m=m, n=n):
                return v.tensor_tensor(r2[:, m, 0:n], ps[3][:, 0:n],
                                       r_p[m % 2][:, 0:n], ALU.mult)

            @p.step(f"resm{C}_{m}", "vector", deps=[f"gt{C}_{m}", f"xld{C}_{m}"])
            def _(v, m=m, n=n):
                return v.tensor_tensor(r2[:, m, 0:n], r2[:, m, 0:n],
                                       x1r[m % 2][:, 0:n], ALU.add)

        if ci + 1 < len(FCHUNKS):
            ffn_front(ci + 1)

        @p.step(f"c2{C}", "scalar", deps=[f"resm{C}_7"] +
                ([f"smt{pf}b"] if pf else []))
        def _(eng, n=n):
            return eng.activation(r2sq[:, :, 0:n], r2[:, :, 0:n], AF.Square)

        @p.step(f"c2v{C}", "vector", deps=[f"resm{C}_7"] +
                ([f"smt{pf}b"] if pf else []))
        def _(v, n=n):
            return v.tensor_scalar(r2b[:, :, 0:n], r2[:, :, 0:n], 1.0, None, ALU.mult)

        sdone2 = stats(f"t{C}", r2b, r2sq, f"c2v{C}",
                       ([f"ap2f{FCHUNKS[ci - 1][0]}"] if ci else ["x1" + CHUNKS[-1][0]]), n,
                       smdep=([f"pptf{FCHUNKS[ci - 1][0]}v"] if ci else [f"ppo{CHUNKS[-1][0]}v"]),
                       depc2=f"c2{C}")

        @p.step(f"ap2{C}", "vector", deps=[sdone2])
        def _(v, n=n):
            bmb = lambda: bm[:, 0:n].unsqueeze(1).broadcast_to([128, 4, n])
            bsb = lambda: bs[:, 0:n].unsqueeze(1).broadcast_to([128, 4, n])
            v.reciprocal(bs[:, 0:n], bs[:, 0:n])
            for h in (slice(0, 4), slice(4, 8)):
                v.tensor_tensor(r2[:, h, 0:n], r2[:, h, 0:n], bmb(), ALU.subtract)
            for h in (slice(0, 4), slice(4, 8)):
                last = v.tensor_tensor(r2[:, h, 0:n], r2[:, h, 0:n],
                                       bsb(), ALU.mult)
            return last

        @p.dma(f"dout{C}", "sync", deps=[f"ap2{C}"])
        def _(eng, dma, off=off, n=n):
            dma(bass.AP(xout, off, [[T, 128], [128 * T, 8], [1, n]]),
                r2[:, :, 0:n])
        dc[f"dout{C}"] = 1

    ffn_front(0)
    for ci in range(len(FCHUNKS)):
        ffn_body(ci)
    prev_last = "doutf3"

    return [f"doutf{cn}" for cn, _, _ in FCHUNKS]


_CACHE = {}


def _build():
    if "prog" in _CACHE:
        return _CACHE["prog"]
    nc = bass.Bass("TRN2", target_bir_lowering=False)
    io = {}
    io["xc"] = nc.dram_tensor("xc", [NTOK * 128], F32, kind="ExternalInput")
    io["ccol"] = nc.dram_tensor("ccol", [128, 8], F32, kind="ExternalInput")
    io["are"] = nc.dram_tensor("are", [H, N], F32, kind="ExternalInput")
    io["aim"] = nc.dram_tensor("aim", [H, N], F32, kind="ExternalInput")
    io["ldt"] = nc.dram_tensor("ldt", [H, 1], F32, kind="ExternalInput")
    io["cre"] = nc.dram_tensor("cre", [H, N], F32, kind="ExternalInput")
    io["cim"] = nc.dram_tensor("cim", [H, N], F32, kind="ExternalInput")
    io["anti"] = nc.dram_tensor("anti", [128, 128], BF16, kind="ExternalInput")
    io["ident"] = nc.dram_tensor("ident", [128, 128], F32, kind="ExternalInput")
    io["iota_t"] = nc.dram_tensor("iota_t", [128, 2048], F32, kind="ExternalInput")
    io["xs"] = nc.dram_tensor("xs", [1024, TH], F32, kind="ExternalInput")
    io["wout"] = nc.dram_tensor("wout", [1024, 2048], F32, kind="ExternalInput")
    io["wr"] = nc.dram_tensor("wr", [1024, 1024], F32, kind="ExternalInput")
    io["wk"] = nc.dram_tensor("wk", [1024, 1024], F32, kind="ExternalInput")
    io["wv"] = nc.dram_tensor("wv", [1024, 1024], F32, kind="ExternalInput")
    io["dvecd"] = nc.dram_tensor("dvecd", [1024], F32, kind="ExternalInput")
    io["bvecd"] = nc.dram_tensor("bvecd", [2048], F32, kind="ExternalInput")
    io["gvecd"] = nc.dram_tensor("gvecd", [1024], F32, kind="ExternalInput")
    io["betvd"] = nc.dram_tensor("betvd", [1024], F32, kind="ExternalInput")
    io["murd"] = nc.dram_tensor("murd", [1024], F32, kind="ExternalInput")
    io["mukd"] = nc.dram_tensor("mukd", [1024], F32, kind="ExternalInput")
    io["consts"] = nc.dram_tensor("consts", [128, 8], F32, kind="ExternalInput")
    io["onesd"] = nc.dram_tensor("onesd", [128, 128], BF16, kind="ExternalInput")
    io["onesfd"] = nc.dram_tensor("onesfd", [1, 128], F32, kind="ExternalInput")
    io["dpar"] = nc.dram_tensor("dpar", [4 * 8192], F32)
    io["dkpad"] = nc.dram_tensor("dkpad", [128 * 256], F32)
    io["dlq"] = nc.dram_tensor("dlq", [128 * 128], F32)
    io["dyb"] = nc.dram_tensor("dyb", [8 * 128 * 2049], BF16)
    io["ga"] = nc.dram_tensor("ga", [8 * 128 * 2049], BF16)
    io["dx1"] = nc.dram_tensor("dx1", [8 * 128 * TH], F32)
    io["dstats"] = nc.dram_tensor("dstats", [32 * 1024], F32)
    io["xout"] = nc.dram_tensor("xout", [1024, T], F32, kind="ExternalOutput")

    ctx = contextlib.ExitStack()
    with ctx:
        p = Prog(nc, ctx)
        dc = {}
        arena = ctx.enter_context(nc.sbuf_tensor("arena", [128, AR_W], F32))
        ps = [ctx.enter_context(nc.psum_tensor(f"ps{i}", [128, 512], F32))
              for i in range(8)]
        build_conv(nc, ctx, p, io, dc, arena, ps)
        douts = build_main(nc, ctx, p, io, dc, arena, ps)

        @p.step("fin", "gpsimd", deps=douts)
        def _(eng):
            return eng.engine_nop()

        block = ctx.enter_context(nc.Block())
        p.build(block, dc)
    _CACHE["prog"] = nc
    return nc


def kernel(L0_attn_x, L0_attn_A_re, L0_attn_A_im, L0_attn_log_dt, L0_attn_C_re,
           L0_attn_C_im, L0_attn_D, L0_attn_W_out, L0_attn_b_out, L0_attn_gamma,
           L0_attn_beta, L0_ffn_W_r, L0_ffn_W_k, L0_ffn_W_v, L0_ffn_mu_r,
           L0_ffn_mu_k):
    f32 = lambda a: np.ascontiguousarray(np.asarray(a, dtype=np.float32))
    x = f32(L0_attn_x)
    A_re = f32(L0_attn_A_re); A_im = f32(L0_attn_A_im)
    log_dt = f32(L0_attn_log_dt)
    C_re = f32(L0_attn_C_re); C_im = f32(L0_attn_C_im)
    D = f32(L0_attn_D)
    W_out = f32(L0_attn_W_out); b_out = f32(L0_attn_b_out)
    gamma = f32(L0_attn_gamma); beta = f32(L0_attn_beta)
    W_rT = np.ascontiguousarray(f32(L0_ffn_W_r).T)
    W_kT = np.ascontiguousarray(f32(L0_ffn_W_k).T)
    W_vT = np.ascontiguousarray(f32(L0_ffn_W_v).T)
    mu_r = f32(L0_ffn_mu_r); mu_k = f32(L0_ffn_mu_k)

    nc = _build()
    in_maps = []
    ones_bf = np.full((128, 128), 1.0 / 1024, ml_dtypes.bfloat16)
    for s in range(8):
        r = s
        ch = slice(128 * r, 128 * (r + 1))
        b, w = s // 4, s % 4
        t0 = w * 2048
        xs_np = np.zeros((1024, TH), np.float32)
        xs_np[:, 1:] = x[b, t0:t0 + 2048].T
        if w > 0:
            xs_np[:, 0] = x[b, t0 - 1]
        cons = np.zeros((128, 8), np.float32)
        cons[:, 0] = 1.0 if w > 0 else 0.0
        cons[:, 1] = 1.0
        cons[:, 2] = 1e-5
        cons[:, 3] = 0.0
        in_maps.append(dict(
            xc=np.ascontiguousarray(x[:, :, ch]).reshape(-1),
            ccol=CCOL_NP, are=A_re[ch], aim=A_im[ch],
            ldt=log_dt[ch][:, None], cre=C_re[ch], cim=C_im[ch],
            anti=ANTI_NP, ident=IDENT_NP, iota_t=IOTA_NP,
            xs=xs_np, wout=W_out, wr=W_rT, wk=W_kT, wv=W_vT,
            dvecd=D, bvecd=b_out, gvecd=gamma, betvd=beta,
            murd=mu_r, mukd=mu_k, consts=cons, onesd=ones_bf,
            onesfd=np.ones((1, 128), np.float32)))

    res = run_bass_kernel_spmd(nc, in_maps, core_ids=list(range(8)))
    out = np.empty((2, 8192, 1024), np.float32)
    for s in range(8):
        b, w = s // 4, s % 4
        t0 = w * 2048
        out[b, t0:t0 + 2048] = res.results[s]["xout"].T
    return out
